# version 02 (final)
# baseline (speedup 1.0000x reference)
import math
import numpy as np
import ml_dtypes
import concourse.bass as bass
import concourse.mybir as mybir
from concourse.bass_utils import run_bass_kernel_spmd
from contextlib import ExitStack

F32 = mybir.dt.float32
BF16 = mybir.dt.bfloat16
I32 = mybir.dt.int32
AF = mybir.ActivationFunctionType
ALU = mybir.AluOpType
AX = mybir.AxisListType
BF = ml_dtypes.bfloat16

ENGS = ("tensor", "scalar", "vector", "gpsimd", "sync")

D_MODEL = 2048
NCHUNK = 16
SEQ = 8192
NCORES = 8
TOK = 2048
ISQ = 1.0 / math.sqrt(128.0)


class _Op:
    __slots__ = ("eng", "meth", "args", "kwargs", "deps", "gid", "kind", "val",
                 "referenced", "waits", "snap", "stream")


class Rec:

    def __init__(self):
        self.ops = {e: [] for e in ENGS}
        self.n = 0
        self.streams = {}
        self.full_wait = set()

    def op(self, eng, meth, *args, deps=(), kind="compute", stream=None, **kwargs):
        o = _Op()
        o.eng, o.meth, o.args, o.kwargs = eng, meth, args, kwargs
        o.deps = [d for d in deps if d is not None]
        o.kind = kind
        o.stream = stream
        o.gid = self.n
        self.n += 1
        o.referenced = False
        o.val = None
        o.waits = []
        o.snap = None
        for d in o.deps:
            assert d.gid < o.gid
        if kind == "dma":
            assert stream is not None
            c = self.streams.get(stream, 0) + 16
            self.streams[stream] = c
            o.val = c
            o.referenced = True
        self.ops[eng].append(o)
        return o

    def dma(self, eng, out, in_, deps=(), stream=None):
        return self.op(eng, "dma_start", out, in_, deps=deps, kind="dma",
                       stream=stream)

    def finalize(self, blk, sems, stream_sems, trunc=None):
        if trunc is not None:
            for e in ENGS:
                self.ops[e] = [o for o in self.ops[e] if o.gid <= trunc]
            self.streams = {}
            for e in ENGS:
                for o in self.ops[e]:
                    if o.kind == "dma":
                        self.streams[o.stream] = max(self.streams.get(o.stream, 0), o.val)
        for e in ENGS:
            for o in self.ops[e]:
                for d in o.deps:
                    if d.eng != o.eng or d.kind == "dma":
                        d.referenced = True
        finals = {}
        for e in ENGS:
            cc = 0
            for o in self.ops[e]:
                if o.kind != "dma" and o.referenced:
                    cc += 1
                    o.val = cc
            finals[e] = cc

        def skey(o):
            if o.kind == "dma":
                return ("S", o.stream)
            return ("E", o.eng)

        clocks = {e: {} for e in ENGS}
        allops = sorted((o for e in ENGS for o in self.ops[e]), key=lambda o: o.gid)
        for o in allops:
            clk = clocks[o.eng]
            for d in o.deps:
                if d.eng == o.eng and d.kind != "dma":
                    continue
                k = skey(d)
                v = d.val
                if d.kind == "dma" and d.stream in self.full_wait:
                    v = self.streams[d.stream]
                if clk.get(k, 0) < v:
                    o.waits.append((k, v))
                    clk[k] = v
                if d.snap:
                    for kk, vv in d.snap.items():
                        if clk.get(kk, 0) < vv:
                            clk[kk] = vv
            if o.referenced:
                o.snap = dict(clk)
                if o.kind != "dma" or o.stream not in self.full_wait:
                    o.snap[skey(o)] = o.val

        stream_eng = {}
        for e in ENGS:
            for o in self.ops[e]:
                if o.kind == "dma":
                    stream_eng.setdefault(o.stream, e)

        def mkbody(ename):
            ops = self.ops[ename]

            def _b(eng):
                for o in ops:
                    for (k, v) in o.waits:
                        sem = stream_sems[k[1]] if k[0] == "S" else sems[k[1]]
                        eng.wait_ge(sem, v)
                    inst = getattr(eng, o.meth)(*o.args, **o.kwargs)
                    if o.kind == "dma":
                        inst.then_inc(stream_sems[o.stream], 16)
                    elif o.referenced:
                        inst.then_inc(sems[o.eng], 1)
                for st, e2 in stream_eng.items():
                    if e2 == ename:
                        eng.wait_ge(stream_sems[st], self.streams[st])
            return _b

        blk.tensor(mkbody("tensor"))
        blk.scalar(mkbody("scalar"))
        blk.vector(mkbody("vector"))
        blk.gpsimd(mkbody("gpsimd"))
        blk.sync(mkbody("sync"))


def new_nc():
    return bass.Bass("TRN2", target_bir_lowering=False)


def open_sems(nc, es):
    return {e: es.enter_context(nc.semaphore(f"sc_{e}")) for e in ENGS}


def open_stream_sems(nc, es, rec):
    return {s: es.enter_context(nc.semaphore(f"sd_{s}")) for s in rec.streams}


class Last:
    def __init__(self, n):
        self.v = [[] for _ in range(n)]

    def get(self, i):
        return self.v[i % len(self.v)]

    def set(self, i, *ops):
        self.v[i % len(self.v)] = [o for o in ops if o is not None]


_TABLES = None


def make_tables():
    global _TABLES
    if _TABLES is not None:
        return _TABLES
    t = {}
    t["ones_bf"] = np.ones((128, 128), dtype=BF)
    t["ones_f32col"] = np.ones((128, 1), dtype=np.float32)
    t["negc128"] = np.full((1, 128), -ISQ, dtype=np.float32)
    t["eye_bf"] = np.eye(128, dtype=np.float32).astype(BF)
    oh4 = np.zeros((4, 128, 4), dtype=np.float32)
    for h in range(4):
        oh4[h, :, h] = 1.0
    t["oh4"] = oh4.astype(BF)
    oh16 = np.zeros((16, 128, 16), dtype=np.float32)
    for h in range(16):
        oh16[h, :, h] = 1.0
    t["oh16"] = oh16.astype(BF)
    masks = np.zeros((4, 128, 512), dtype=np.float32)
    for m in range(4):
        for i in range(128):
            masks[m, i, 128 * m + i:] = 1.0
    t["masks"] = masks.astype(BF)
    D = 128
    inv_freq = (1.0 / (10000.0 ** (np.arange(0, D, 2, dtype=np.float32) / D))).astype(np.float32)
    tt = np.arange(SEQ, dtype=np.float32)
    freqs = np.outer(tt, inv_freq).astype(np.float32)
    emb = np.concatenate([freqs, freqs], axis=-1)
    t["cos_t"] = np.ascontiguousarray(np.cos(emb).T)
    t["sin_t"] = np.ascontiguousarray(np.sin(emb).T)
    H = np.array([[1.0]], dtype=np.float32)
    while H.shape[0] < 128:
        H = np.block([[H, H], [H, -H]])
    t["h128"] = H.astype(BF)
    n = np.arange(128)
    ang = 2.0 * np.pi * np.outer(n, n) / 128.0
    t["dftC"] = np.cos(ang).astype(np.float32).astype(BF)
    t["dftSneg"] = (-np.sin(ang)).astype(np.float32).astype(BF)
    t["dftS"] = np.sin(ang).astype(np.float32).astype(BF)
    _TABLES = t
    return t


def varB_dummy(x):
    return x


def fm(x):
    return np.ascontiguousarray(np.asarray(x).T)


def build_p1(stages="ABC", ntiles=16, parts="NVGB", trunc=None):
    nlev = 0
    if "N" in parts:
        i = parts.index("N")
        nlev = 4
        if i + 1 < len(parts) and parts[i + 1].isdigit():
            nlev = int(parts[i + 1])
    nc = new_nc()
    x = nc.dram_tensor("x_seq", [D_MODEL, SEQ], F32, kind="ExternalInput").ap()
    wq = nc.dram_tensor("wqT", [D_MODEL, 512], F32, kind="ExternalInput").ap()
    wk = nc.dram_tensor("wkT", [D_MODEL, 128], F32, kind="ExternalInput").ap()
    wv = nc.dram_tensor("wvT", [D_MODEL, 128], F32, kind="ExternalInput").ap()
    wgs = nc.dram_tensor("wgslcT", [D_MODEL, 4], F32, kind="ExternalInput").ap()
    wgw = nc.dram_tensor("wgswaT", [D_MODEL, 4], F32, kind="ExternalInput").ap()
    cos_t = nc.dram_tensor("cos_t", [128, SEQ], F32, kind="ExternalInput").ap()
    sin_t = nc.dram_tensor("sin_t", [128, SEQ], F32, kind="ExternalInput").ap()
    masks = nc.dram_tensor("masks", [4, 128, 512], BF16, kind="ExternalInput").ap()
    oh4 = nc.dram_tensor("oh4", [4, 128, 4], BF16, kind="ExternalInput").ap()
    eye = nc.dram_tensor("eye_bf", [128, 128], BF16, kind="ExternalInput").ap()
    onesb = nc.dram_tensor("ones_bf", [128, 128], BF16, kind="ExternalInput").ap()
    onesc = nc.dram_tensor("ones_f32col", [128, 1], F32, kind="ExternalInput").ap()
    negc = nc.dram_tensor("negc128", [1, 128], F32, kind="ExternalInput").ap()

    out = nc.dram_tensor("attnout", [512, SEQ], BF16, kind="ExternalOutput").ap()
    g_dram = nc.dram_tensor("g_scr", [4, SEQ], F32).ap()
    rows_dram = nc.dram_tensor("rows_scr", [2, SEQ], BF16).ap()
    gdrow_dram = nc.dram_tensor("gdrow_scr", [2, 512], F32).ap()

    es = ExitStack()
    sems = open_sems(nc, es)
    sb = lambda name, shape, dt: es.enter_context(nc.sbuf_tensor(name, shape, dt))

    xstage = [sb(f"xstage{i}", [128, 512], F32) for i in range(2)]
    xbf = sb("xbf", [128, 16 * 512], BF16)
    xt = sb("xt", [128, 16 * 512], BF16)
    q_bf = sb("q_bf", [128, 4 * SEQ], BF16)
    k_bf = sb("k_bf", [128, SEQ], BF16)
    v_tm = sb("v_tm", [128, SEQ], BF16)
    wq_s = sb("wq_s", [128, 16 * 512], BF16)
    wk_s = sb("wk_s", [128, 16 * 128], BF16)
    wv_s = sb("wv_s", [128, 16 * 128], BF16)
    wg_s = sb("wg_s", [128, 16 * 8], BF16)
    sqb = [sb(f"sqb{i}", [128, 512], BF16) for i in range(2)]
    cosb = [sb("cosb0", [128, 512], F32)] * 2
    sinb = [sb("sinb0", [128, 512], F32)] * 2
    t_a = sb("t_a", [128, 512], BF16)
    t_b = sb("t_b", [128, 512], BF16)
    vfm = sb("vfm", [128, 512], BF16)
    rrow_bf = sb("rrow_bf", [1, 512], BF16)
    nmr_bf = sb("nmr_bf", [1, 512], BF16)
    rbc = sb("rbc", [128, 512], BF16)
    nbc = sb("nbc", [128, 512], BF16)
    epst = sb("epst", [1, 1], F32)
    mnrow = sb("mnrow", [1, 512], F32)
    rowA = sb("rowA", [1, 512], F32)
    rowB = sb("rowB", [1, 512], F32)
    g4 = sb("g4", [4, 512], F32)
    g4b = sb("g4b", [4, 512], F32)
    qmax2 = sb("qmax2", [4, 1], F32)
    kmax2 = sb("kmax2", [1, 1], F32)
    redt = sb("redt", [4, 1], F32)
    redk = sb("redk", [1, 1], F32)
    qm = sb("qm", [4, 1], F32)
    km = sb("km", [1, 1], F32)
    qmT = sb("qmT", [1, 4], F32)
    prodT = sb("prodT", [1, 4], F32)
    bias4 = sb("bias4", [128, 4], F32)
    masks_s = sb("masks_s", [128, 4 * 512], BF16)
    oh4_s = sb("oh4_s", [128, 4 * 4], BF16)
    eye_s = sb("eye_s", [128, 128], BF16)
    onesb_s = sb("onesb_s", [128, 128], BF16)
    onesc_s = sb("onesc_s", [128, 1], F32)
    negc_s = sb("negc_s", [1, 128], F32)
    pbuf = [sb(f"pbuf{i}", [128, 512], BF16) for i in range(3)]
    dacc = [sb("dacc0", [128, 512], F32)] * 2
    rdrow = [sb(f"rdrow{i}", [1, 512], F32) for i in range(2)]
    gdrow = rdrow
    growt = [sb(f"growt{i}", [1, 512], F32) for i in range(2)]
    gbc = [sb("gbc0", [128, 512], F32)] * 2
    out_t = [sb("out_t0", [128, 512], BF16)] * 2

    r = Rec()

    eps_set = r.op("vector", "memset", epst[:, :], 1e-5)

    ld_masks = r.dma("sync", masks_s.rearrange("p (m n) -> p m n", m=4), masks.transpose([1, 0, 2]), stream="const")
    ld_oh4 = r.dma("sync", oh4_s.rearrange("p (m n) -> p m n", m=4), oh4.transpose([1, 0, 2]), stream="const")
    ld_eye = r.dma("sync", eye_s[:, :], eye, stream="const")
    ld_onesb = r.dma("sync", onesb_s[:, :], onesb, stream="const")
    ld_onesc = r.dma("sync", onesc_s[:, :], onesc, stream="const")
    ld_negc = r.dma("sync", negc_s[:, :], negc, stream="const")

    wlast = [None, None]
    wdeps = []
    for (wap, cols, dst) in ((wq, 512, wq_s), (wk, 128, wk_s), (wv, 128, wv_s)):
        for m in range(cols // 128):
            for qtr in range(4):
                buf = xstage[qtr % 2]
                src = wap[qtr * 512:(qtr + 1) * 512, m * 128:(m + 1) * 128].rearrange(
                    "(kc p) j -> p kc j", p=128)
                ld = r.dma("sync", buf.rearrange("p (kc j) -> p kc j", kc=4), src,
                           deps=[wlast[qtr % 2]], stream=f"xst{qtr % 2}")
                if cols == 512:
                    dstv = wq_s.rearrange("p (kc m j) -> p kc m j", kc=16, m=4)[:, qtr * 4:qtr * 4 + 4, m, :]
                else:
                    dstv = dst.rearrange("p (kc j) -> p kc j", kc=16)[:, qtr * 4:qtr * 4 + 4, :]
                cvw = r.op("vector", "tensor_copy", dstv,
                           buf.rearrange("p (kc j) -> p kc j", kc=4), deps=[ld])
                wlast[qtr % 2] = cvw
                wdeps.append(cvw)
    for gi, wap in enumerate((wgs, wgw)):
        for qtr in range(4):
            buf = xstage[qtr % 2]
            src = wap[qtr * 512:(qtr + 1) * 512, :].rearrange("(kc p) j -> p kc j", p=128)
            ld = r.dma("sync", buf.rearrange("p (kc j) -> p kc j", kc=4)[:, :, 0:4], src,
                       deps=[wlast[qtr % 2]], stream=f"xst{qtr % 2}")
            cvw = r.op("vector", "tensor_copy",
                       wg_s.rearrange("p (kc j) -> p kc j", kc=16, j=8)[:, qtr * 4:qtr * 4 + 4, gi * 4:(gi + 1) * 4],
                       buf.rearrange("p (kc j) -> p kc j", kc=4)[:, :, 0:4], deps=[ld])
            wlast[qtr % 2] = cvw
            wdeps.append(cvw)

    esA = ExitStack()
    psA = lambda name, shape, dt: esA.enter_context(nc.psum_tensor(name, shape, dt))
    ps_proj = [psA(f"ps_proj{i}", [128, 512], F32) for i in range(2)]
    ps_sum = psA("ps_sum", [1, 512], F32)
    ps_sq = psA("ps_sq", [1, 512], F32)
    ps_g = psA("ps_g", [4, 512], F32)
    ps_qn = psA("ps_qn", [4, 512], F32)
    ps_kn = psA("ps_kn", [1, 512], F32)
    ps_tr = psA("ps_tr", [128, 128], BF16)

    NT = ntiles
    L_xstage = Last(2)
    L_xstage.set(0, wlast[0])
    L_xstage.set(1, wlast[1])
    L_xbf = Last(1)
    L_xt = Last(1)
    L_sqb = Last(2)
    L_psproj = Last(2)
    L_stats = Last(1)
    L_qn = Last(1)
    L_kn = Last(1)
    L_rows = Last(1)
    L_mnrow = Last(1)
    L_rowA = Last(1)
    L_rowB = Last(1)
    L_rowsd = Last(1)
    L_cos = Last(1)
    L_vfm = Last(1)
    L_tr = Last(1)
    L_g = Last(1)
    qmax_prev = None
    kmax_prev = None
    last_gadd = None
    pp = 0

    def MMgroup(psap, pairs, deps, first_extra=()):
        n = len(pairs)
        last = None
        for i, (lh, rh) in enumerate(pairs):
            dd = list(deps) if i == 0 else []
            if i == 0:
                dd += list(first_extra)
            last = r.op("tensor", "matmul", psap, lh, rh,
                        start=(i == 0), stop=(i == n - 1), deps=dd)
        return last

    for ti in range(NT):
        tsl = slice(ti * 512, (ti + 1) * 512)
        cv_ops = []
        for gi in range(16):
            xs = xstage[gi % 2]
            ld = r.dma("sync", xs[:, :], x[gi * 128:(gi + 1) * 128, tsl],
                       deps=L_xstage.get(gi), stream=f"xst{gi % 2}")
            cv = r.op("vector", "tensor_copy", xbf[:, gi * 512:(gi + 1) * 512],
                      xs[:, :], deps=[ld] + L_xbf.get(0))
            L_xstage.set(gi, cv)
            cv_ops.append(cv)
        sum_last = None
        sq_last = None
        for c in range(16):
            gi = c
            dd = [cv_ops[gi]]
            if c == 0:
                dd += [ld_onesb] + L_stats.get(0)
            sum_last = r.op("tensor", "matmul", ps_sum[:, :], onesb_s[:, 0:1],
                            xbf[:, c * 512:(c + 1) * 512],
                            start=(c == 0), stop=(c == 15), deps=dd)
            sq = r.op("scalar", "activation", sqb[c % 2][:, :],
                      xbf[:, c * 512:(c + 1) * 512], AF.Square,
                      deps=[cv_ops[gi]] + L_sqb.get(c))
            dd2 = [sq]
            if c == 0:
                dd2 += L_stats.get(0)
            sq_last = r.op("tensor", "matmul", ps_sq[:, :], onesb_s[:, 0:1],
                           sqb[c % 2][:, :], start=(c == 0), stop=(c == 15), deps=dd2)
            L_sqb.set(c, sq_last)
        mn = r.op("scalar", "mul", mnrow[:, :], ps_sum[:, :], -1.0 / 2048.0,
                  deps=[sum_last] + L_mnrow.get(0))
        e2 = r.op("scalar", "mul", rowA[:, :], ps_sq[:, :], 1.0 / 2048.0,
                  deps=[sq_last] + L_rowA.get(0))
        m2 = r.op("scalar", "activation", rowB[:, :], mnrow[:, :], AF.Square,
                  deps=[mn] + L_rowB.get(0))
        var = r.op("vector", "tensor_sub", rowA[:, :], rowA[:, :], rowB[:, :], deps=[e2, m2])
        sd = r.op("scalar", "activation", rowB[:, :], varB_dummy(rowA)[:, :], AF.Sqrt, bias=epst[:, :], deps=[var, m2, eps_set])
        rr = r.op("vector", "reciprocal", rowA[:, :], rowB[:, :], deps=[sd, var])
        nm = r.op("vector", "tensor_mul", rowB[:, :], mnrow[:, :], rowA[:, :], deps=[rr, mn, sd])
        L_stats.set(0, mn, e2)
        rb = r.op("scalar", "copy", rrow_bf[:, :], rowA[:, :], deps=[rr] + L_rowsd.get(0))
        nb = r.op("scalar", "copy", nmr_bf[:, :], rowB[:, :], deps=[nm] + L_rowsd.get(0))
        L_mnrow.set(0, nm)
        L_rowA.set(0, nb if False else rb, nm)
        L_rowB.set(0, nb)
        st1 = r.dma("gpsimd", rows_dram[0:1, tsl], rrow_bf[:, :], deps=[rb], stream="rows0")
        st2 = r.dma("gpsimd", rows_dram[1:2, tsl], nmr_bf[:, :], deps=[nb], stream="rows1")
        bc1 = r.dma("gpsimd", rbc[:, :], rows_dram[0:1, tsl].broadcast_to((128, 512)),
                    deps=[st1] + L_rows.get(0), stream="rbc")
        bc2 = r.dma("gpsimd", nbc[:, :], rows_dram[1:2, tsl].broadcast_to((128, 512)),
                    deps=[st2] + L_rows.get(0), stream="nbc")
        L_rowsd.set(0, st1, st2)
        apply_ops = []
        for c in range(16):
            gi = c
            p1o = r.op("vector", "tensor_mul", xt[:, c * 512:(c + 1) * 512],
                       xbf[:, c * 512:(c + 1) * 512], rbc[:, :],
                       deps=[bc1, cv_ops[gi]] + L_xt.get(0))
            p2o = r.op("vector", "tensor_add", xt[:, c * 512:(c + 1) * 512],
                       xt[:, c * 512:(c + 1) * 512], nbc[:, :], deps=[bc2, p1o])
            apply_ops.append(p2o)
        L_xbf.set(0, *apply_ops[-2:])
        L_rows.set(0, *apply_ops[-2:])
        L_xt.set(0)
        xt_cons = []

        rhs_all = [xt[:, kc * 512:(kc + 1) * 512] for kc in range(16)]
        ldc = r.dma("sync", cosb[0][:, :], cos_t[:, tsl], deps=L_cos.get(0), stream="cosb")
        lds = r.dma("sync", sinb[0][:, :], sin_t[:, tsl], deps=L_cos.get(0), stream="sinb")
        rope_cons = []

        def do_rope(mm, ps_, dst_ap):
            cb = cosb[0]
            sn = sinb[0]
            ta1 = r.op("vector", "tensor_mul", t_a[0:64, :], ps_[64:128, :], sn[0:64, :], deps=[mm, lds])
            ta2 = r.op("vector", "tensor_mul", t_a[64:128, :], ps_[0:64, :], sn[64:128, :], deps=[mm, lds])
            tb = r.op("vector", "tensor_mul", t_b[:, :], ps_[:, :], cb[:, :], deps=[mm, ldc])
            o1 = r.op("vector", "tensor_sub", dst_ap[0:64, :], t_b[0:64, :], t_a[0:64, :], deps=[ta1, tb])
            o2 = r.op("vector", "tensor_add", dst_ap[64:128, :], t_b[64:128, :], t_a[64:128, :], deps=[ta2, tb])
            rope_cons.extend([o1, o2])
            return o2

        qn_last = None
        for m in range(4):
            pairs = [(wq_s[:, (kc * 4 + m) * 128:(kc * 4 + m) * 128 + 128], rhs_all[kc])
                     for kc in range(16)]
            mm = MMgroup(ps_proj[pp][:, :], pairs,
                         deps=apply_ops + (wdeps if ti == 0 else []),
                         first_extra=L_psproj.get(pp))
            xt_cons.append(mm)
            o2 = do_rope(mm, ps_proj[pp], q_bf[:, m * SEQ + ti * 512: m * SEQ + ti * 512 + 512])
            if "N" not in parts:
                L_psproj.set(pp, o2)
                pp ^= 1
                continue
            sqq = r.op("scalar", "activation", sqb[m % 2][:, :],
                       q_bf[:, m * SEQ + ti * 512: m * SEQ + ti * 512 + 512], AF.Square,
                       deps=rope_cons[-2:] + L_sqb.get(m))
            if nlev >= 2:
                dd = [sqq, ld_oh4]
                if m == 0:
                    dd += L_qn.get(0)
                qn_last = r.op("tensor", "matmul", ps_qn[:, :], oh4_s[:, m * 4:(m + 1) * 4],
                               sqb[m % 2][:, :], start=(m == 0), stop=(m == 3), deps=dd)
                L_sqb.set(m, qn_last)
            else:
                L_sqb.set(m, sqq)
            L_psproj.set(pp, o2)
            pp ^= 1
        rmax = None
        if nlev >= 3:
            rmax = r.op("vector", "reduce_max", redt.rearrange("p (a b) -> p a b", b=1),
                        ps_qn.rearrange("p (a b) -> p a b", b=512), axis=AX.X, deps=[qn_last])
            L_qn.set(0, rmax)
        d1 = r.op("vector", "memset", t_a[:, 0:64], 0.0, deps=[])
        d2 = r.op("vector", "memset", t_a[:, 0:64], 0.0, deps=[])
        if nlev >= 4:
            if ti == 0:
                qmax_prev = r.op("vector", "tensor_copy", qmax2[:, :], redt[:, :],
                                 deps=[rmax, d2])
            else:
                qmax_prev = r.op("vector", "tensor_max", qmax2[:, :], qmax2[:, :], redt[:, :],
                                 deps=[rmax, d2, qmax_prev])

        pairs = [(wk_s[:, kc * 128:(kc + 1) * 128], rhs_all[kc]) for kc in range(16)]
        mmk = MMgroup(ps_proj[pp][:, :], pairs, deps=apply_ops, first_extra=L_psproj.get(pp))
        xt_cons.append(mmk)
        o2k = do_rope(mmk, ps_proj[pp], k_bf[:, tsl])
        if "N" not in parts:
            L_psproj.set(pp, o2k)
            pp ^= 1
        else:
            sqk = r.op("scalar", "activation", sqb[0][:, :], k_bf[:, tsl], AF.Square,
                       deps=rope_cons[-2:] + L_sqb.get(0))
            L_psproj.set(pp, o2k)
            pp ^= 1
            if nlev >= 2:
                kn = r.op("tensor", "matmul", ps_kn[:, :], onesb_s[:, 0:1], sqb[0][:, :],
                          start=True, stop=True, deps=[sqk, ld_onesb] + L_kn.get(0))
                L_sqb.set(0, kn)
            else:
                L_sqb.set(0, sqk)
            if nlev >= 3:
                rmaxk = r.op("vector", "reduce_max", redk.rearrange("p (a b) -> p a b", b=1),
                             ps_kn.rearrange("p (a b) -> p a b", b=512), axis=AX.X, deps=[kn])
                L_kn.set(0, rmaxk)
            d3 = r.op("vector", "memset", t_a[:, 0:64], 0.0, deps=[])
            d4 = r.op("vector", "memset", t_a[:, 0:64], 0.0, deps=[])
            if nlev >= 4:
                if ti == 0:
                    kmax_prev = r.op("vector", "tensor_copy", kmax2[:, :], redk[:, :],
                                     deps=[rmaxk, d4])
                else:
                    kmax_prev = r.op("vector", "tensor_max", kmax2[:, :], kmax2[:, :], redk[:, :],
                                     deps=[rmaxk, d4, kmax_prev])

        if "V" not in parts:
            pairs = []
        else:
            pairs = [(wv_s[:, kc * 128:(kc + 1) * 128], rhs_all[kc]) for kc in range(16)]
        if pairs:
            mmv = MMgroup(ps_proj[pp][:, :], pairs, deps=apply_ops, first_extra=L_psproj.get(pp))
            xt_cons.append(mmv)
            vc = r.op("vector", "tensor_copy", vfm[:, :], ps_proj[pp][:, :],
                      deps=[mmv] + L_vfm.get(0))
            L_psproj.set(pp, vc)
            pp ^= 1
            tr_evs = []
            for j in range(4):
                tr = r.op("tensor", "transpose", ps_tr[:, :], vfm[:, j * 128:(j + 1) * 128],
                          eye_s[:, :], deps=[vc, ld_eye] + L_tr.get(0))
                ev = r.op("vector", "tensor_copy",
                          v_tm[:, (ti * 4 + j) * 128:(ti * 4 + j) * 128 + 128], ps_tr[:, :],
                          deps=[tr])
                L_tr.set(0, ev)
                tr_evs.append(ev)
            L_vfm.set(0, tr_evs[-1])

        if "G" in parts:
            pairs = [(wg_s[:, kc * 8:kc * 8 + 4], rhs_all[kc]) for kc in range(16)]
            mg1 = MMgroup(ps_g[:, :], pairs, deps=apply_ops, first_extra=L_g.get(0))
            sg1 = r.op("scalar", "activation", g4[:, :], ps_g[:, :], AF.Sigmoid,
                       deps=[mg1] + L_g.get(0))
            xt_cons.append(mg1)
            pairs = [(wg_s[:, kc * 8 + 4:kc * 8 + 8], rhs_all[kc]) for kc in range(16)]
            mg2 = MMgroup(ps_g[:, :], pairs, deps=apply_ops, first_extra=[sg1])
            xt_cons.append(mg2)
            sg2 = r.op("scalar", "activation", g4b[:, :], ps_g[:, :], AF.Sigmoid,
                       deps=[mg2, last_gadd])
            gadd = r.op("vector", "tensor_add", g4[:, :], g4[:, :], g4b[:, :], deps=[sg1, sg2])
            gdma = r.dma("gpsimd", g_dram[:, tsl], g4[:, :], deps=[gadd], stream="gdr")
            L_g.set(0, gdma, sg2)
            last_gadd = gadd
        else:
            last_gadd = rope_cons[-1]
        L_cos.set(0, *rope_cons[-2:])
        L_xt.set(0, *xt_cons)

    if "B" not in parts:
        bias_done = last_gadd
        dbg_ops2 = [r.dma("sync", out[0:128, 0:512], xt[:, 0:512], deps=[last_gadd], stream="outs")]
        with nc.Block() as blk:
            r.finalize(blk, sems, open_stream_sems(nc, es, r))
        es.close()
        return nc
    sq1 = r.op("scalar", "activation", qm[:, :], qmax2[:, :], AF.Sqrt, deps=[qmax_prev])
    sk1 = r.op("scalar", "activation", km[:, :], kmax2[:, :], AF.Sqrt, deps=[kmax_prev])
    dq1 = r.dma("gpsimd", gdrow_dram[0:1, 0:4].rearrange("one n -> n one"), qm[:, :],
                deps=[sq1], stream="qmt")
    dq = r.dma("gpsimd", qmT[:, :], gdrow_dram[0:1, 0:4], deps=[dq1], stream="qmt")
    prT = r.op("vector", "tensor_scalar", prodT[:, :], qmT[:, :], km[:, :], 5.65685424949238,
               ALU.mult, ALU.add, deps=[dq, sk1])
    esA.close()
    ps_bias = es.enter_context(nc.psum_tensor("ps_bias", [128, 1], F32))
    bias_ops = []
    for h in range(4):
        bm = r.op("tensor", "matmul", ps_bias[:, :], negc_s[:, :], prodT[0:1, h:h + 1],
                  start=True, stop=True, deps=[prT, ld_negc] + bias_ops[-1:])
        be = r.op("scalar", "copy", bias4[:, h:h + 1], ps_bias[:, :], deps=[bm])
        bias_ops.append(be)
    bias_done = bias_ops[-1]

    if "C" not in stages:
        dbg_ops = [qmax_prev, kmax_prev, bias_done]
        for m in range(4):
            dbg_ops.append(r.dma("sync", out[m * 128:(m + 1) * 128, :],
                                 q_bf[:, m * SEQ:(m + 1) * SEQ], deps=[last_gadd, bias_done],
                                 stream="outs"))
        r.full_wait.update(["const"])
        if trunc is not None:
            for e in ENGS:
                r.ops[e] = [o for o in r.ops[e] if o.gid <= trunc]
            r.streams = {}
            for e in ENGS:
                for o in r.ops[e]:
                    if o.kind == "dma":
                        r.streams[o.stream] = max(r.streams.get(o.stream, 0), o.val)
        ssems = open_stream_sems(nc, es, r)
        with nc.Block() as blk:
            r.finalize(blk, sems, ssems)
        es.close()
        return nc

    ps_st = [es.enter_context(nc.psum_tensor(f"ps_st{i}", [128, 512], F32)) for i in range(2)]
    ps_av = [es.enter_context(nc.psum_tensor(f"ps_av{i}", [128, 512], F32)) for i in range(2)]
    ps_ds = es.enter_context(nc.psum_tensor("ps_ds", [1, 512], F32))

    units = []
    for h in range(4):
        for J in range(NT):
            for kt in range(4 * J + 4):
                units.append((h, J, kt))
    NU = len(units)

    hist_ex = {}
    hist_src = {}
    hist_da = {}
    hist_av = {}
    L_dacc = Last(1)
    L_davw = Last(2)
    L_psds = Last(1)
    L_av = Last(2)
    L_grow = Last(2)
    L_gdrow = Last(2)
    L_gdd = Last(2)
    L_gbc = Last(1)
    L_out = Last(1)
    Jpar = {}
    jidx = 0

    def rec_av(u):
        (h, J, kt) = units[u]
        dd = [hist_src[u]]
        if kt == 0:
            dd += L_av.get(J + h * NT)
        av = r.op("tensor", "matmul", ps_av[(J + h * NT) % 2][:, :],
                  v_tm[:, kt * 128:(kt + 1) * 128], pbuf[u % 3][:, :],
                  start=(kt == 0), stop=(kt == 4 * J + 3), deps=dd)
        hist_av[u] = av
        return av

    L_gdd.set(0, dq)

    def rec_epilog(u):
        (h, J, kt) = units[u]
        assert kt == 4 * J + 3
        gj = h * NT + J
        da = hist_da[u]
        ds = r.op("tensor", "matmul", ps_ds[:, :], onesc_s[:, :], dacc[gj % 2][:, :],
                  start=True, stop=True, deps=[da, ld_onesc] + L_psds.get(0))
        L_dacc.set(gj, ds)
        gld = r.dma("sync", growt[gj % 2][:, :], g_dram[h:h + 1, J * 512:J * 512 + 512],
                    deps=[gdma] + L_grow.get(gj), stream=f"grow{gj % 2}")
        rd = r.op("vector", "reciprocal", rdrow[gj % 2][:, :], ps_ds[:, :],
                  deps=[ds] + L_gdrow.get(gj))
        L_psds.set(0, rd)
        gd = r.op("vector", "tensor_mul", gdrow[gj % 2][:, :], rdrow[gj % 2][:, :],
                  growt[gj % 2][:, :], deps=[rd, gld])
        L_grow.set(gj, gd)
        gst = r.dma("gpsimd", gdrow_dram[gj % 2:gj % 2 + 1, :], gdrow[gj % 2][:, :],
                    deps=[gd] + L_gdd.get(gj), stream=f"gdw{gj % 2}")
        gbc_d = r.dma("gpsimd", gbc[gj % 2][:, :],
                      gdrow_dram[gj % 2:gj % 2 + 1, :].broadcast_to((128, 512)),
                      deps=[gst] + L_gbc.get(0), stream="gbc")
        L_gdrow.set(gj, gst)
        L_gdd.set(gj, gbc_d)
        osc = r.op("vector", "tensor_mul", out_t[gj % 2][:, :], ps_av[gj % 2][:, :],
                   gbc[gj % 2][:, :], deps=[gbc_d, hist_av[u]] + L_out.get(gj))
        L_gbc.set(0, osc)
        L_av.set(gj, osc)
        stt = r.dma("sync", out[h * 128:(h + 1) * 128, J * 512:J * 512 + 512],
                    out_t[gj % 2][:, :], deps=[osc], stream="outs")
        L_out.set(gj, stt)

    for u in range(NU):
        (h, J, kt) = units[u]
        gj = h * NT + J
        dd = []
        if u - 2 in hist_ex:
            dd.append(hist_ex[u - 2])
        if u == 0:
            dd.append(last_gadd)
        mmS = r.op("tensor", "matmul", ps_st[u % 2][:, :],
                   k_bf[:, kt * 128:(kt + 1) * 128],
                   q_bf[:, h * SEQ + J * 512: h * SEQ + J * 512 + 512],
                   start=True, stop=True, deps=dd)
        if u > 0:
            rec_av(u - 1)
            (ph, pJ, pkt) = units[u - 1]
            if pkt == 4 * pJ + 3:
                rec_epilog(u - 1)
        war = []
        if u - 3 in hist_av:
            war = [hist_av[u - 3], hist_da[u - 3]]
        ex = r.op("scalar", "activation", pbuf[u % 3][:, :], ps_st[u % 2][:, :], AF.Exp,
                  bias=bias4[:, h:h + 1], scale=ISQ, deps=[mmS, bias_done] + war)
        hist_ex[u] = ex
        src = ex
        m = kt - 4 * J
        if m >= 0:
            src = r.op("vector", "tensor_mul", pbuf[u % 3][:, :], pbuf[u % 3][:, :],
                       masks_s[:, m * 512:(m + 1) * 512], deps=[ex, ld_masks])
        hist_src[u] = src
        deng = "gpsimd" if (kt % 2) else "vector"
        if kt == 0:
            da = r.op(deng, "tensor_copy", dacc[gj % 2][:, :], pbuf[u % 3][:, :],
                      deps=[src] + L_dacc.get(gj))
        else:
            da = r.op(deng, "tensor_add", dacc[gj % 2][:, :], dacc[gj % 2][:, :],
                      pbuf[u % 3][:, :], deps=[src, hist_da[u - 1]])
        hist_da[u] = da
    rec_av(NU - 1)
    rec_epilog(NU - 1)

    r.full_wait.update(["const"])
    if trunc is not None:
        for e in ENGS:
            r.ops[e] = [o for o in r.ops[e] if o.gid <= trunc]
        r.streams = {}
        for e in ENGS:
            for o in r.ops[e]:
                if o.kind == "dma":
                    r.streams[o.stream] = max(r.streams.get(o.stream, 0), o.val)
    ssems = open_stream_sems(nc, es, r)
    with nc.Block() as blk:
        r.finalize(blk, sems, ssems)
    es.close()
    return nc


def p1_inputs(full_inputs):
    t = make_tables()
    x = np.asarray(full_inputs["L0_attn_hidden_states"])
    wq = np.asarray(full_inputs["L0_attn_Wq"])
    wk = np.asarray(full_inputs["L0_attn_Wk"])
    wv = np.asarray(full_inputs["L0_attn_Wv"])
    wgs = np.asarray(full_inputs["L0_attn_Wgslc"])
    wgw = np.asarray(full_inputs["L0_attn_Wgswa"])
    maps = []
    for cid in range(NCORES):
        s, g = cid // 4, cid % 4
        m = {
            "x_seq": fm(x[s]),
            "wqT": np.ascontiguousarray(wq.T[:, 512 * g:512 * (g + 1)]),
            "wkT": np.ascontiguousarray(wk.T[:, 128 * g:128 * (g + 1)]),
            "wvT": np.ascontiguousarray(wv.T[:, 128 * g:128 * (g + 1)]),
            "wgslcT": np.ascontiguousarray(wgs.T[:, 4 * g:4 * (g + 1)]),
            "wgswaT": np.ascontiguousarray(wgw.T[:, 4 * g:4 * (g + 1)]),
            "cos_t": t["cos_t"], "sin_t": t["sin_t"], "masks": t["masks"],
            "oh4": t["oh4"], "eye_bf": t["eye_bf"], "ones_bf": t["ones_bf"],
            "ones_f32col": t["ones_f32col"], "negc128": t["negc128"],
        }
        maps.append(m)
    return maps


_PROGS = {}


def _prog(name, builder):
    if name not in _PROGS:
        _PROGS[name] = builder()
    return _PROGS[name]


def kernel(**inputs):
    nc1 = _prog("p1", build_p1)
    res1 = run_bass_kernel_spmd(nc1, p1_inputs(inputs), core_ids=list(range(NCORES)))
    p1outs = [np.asarray(res1.results[c]["attnout"]) for c in range(NCORES)]
    nc2 = _prog("p2", build_p2)
    res2 = run_bass_kernel_spmd(nc2, p2_inputs(inputs, p1outs), core_ids=list(range(NCORES)))
    y1s = [np.asarray(res2.results[c]["y1"]) for c in range(NCORES)]
    nc3 = _prog("p3", build_p3)
    res3 = run_bass_kernel_spmd(nc3, p3_inputs(inputs, y1s), core_ids=list(range(NCORES)))
    y2s = [np.asarray(res3.results[c]["y2"]) for c in range(NCORES)]
    nc4 = _prog("p4", build_p4)
    res4 = run_bass_kernel_spmd(nc4, p4_inputs(inputs, y2s), core_ids=list(range(NCORES)))
    tots = [np.asarray(res4.results[c]["tot"]) for c in range(NCORES)]
    nc5 = _prog("p5", build_p5)
    res5 = run_bass_kernel_spmd(nc5, p5_inputs(inputs, y2s, tots), core_ids=list(range(NCORES)))
    y3s = [np.asarray(res5.results[c]["y3"]) for c in range(NCORES)]
    nc6 = _prog("p6", build_p6)
    res6 = run_bass_kernel_spmd(nc6, p6_inputs(inputs, y3s), core_ids=list(range(NCORES)))
    out = np.empty((2, SEQ, D_MODEL), dtype=np.float32)
    for cid in range(NCORES):
        s = cid // 4
        t0 = (cid % 4) * TOK
        out[s, t0:t0 + TOK, :] = np.asarray(res6.results[cid]["y4"]).T
    return out


def build_p2():
    nc = new_nc()
    att = nc.dram_tensor("attn_t", [D_MODEL, TOK], BF16, kind="ExternalInput").ap()
    x = nc.dram_tensor("x_tok", [D_MODEL, TOK], F32, kind="ExternalInput").ap()
    wo = nc.dram_tensor("woT", [D_MODEL, D_MODEL], F32, kind="ExternalInput").ap()
    onesb = nc.dram_tensor("ones_bf", [128, 128], BF16, kind="ExternalInput").ap()
    y1 = nc.dram_tensor("y1", [D_MODEL, TOK], F32, kind="ExternalOutput").ap()
    rows_dram = nc.dram_tensor("rows_scr", [2, TOK], BF16).ap()

    es = ExitStack()
    sems = open_sems(nc, es)
    sb = lambda name, shape, dt: es.enter_context(nc.sbuf_tensor(name, shape, dt))
    ps = lambda name, shape, dt: es.enter_context(nc.psum_tensor(name, shape, dt))

    wstage = [sb(f"wst{i}", [128, 2048], F32) for i in range(2)]
    wo_s = sb("wo_s", [128, 16 * 2048], BF16)
    att_s = sb("att_s", [128, 16 * 512], BF16)
    ao_bf = sb("ao_bf", [128, 16 * 512], BF16)
    xstage = [sb(f"xs{i}", [128, 512], F32) for i in range(2)]
    y1t = [sb(f"y1t{i}", [128, 512], F32) for i in range(2)]
    sqb = [sb(f"sqb{i}", [128, 512], BF16) for i in range(2)]
    onesb_s = sb("onesb_s", [128, 128], BF16)
    epst = sb("epst", [1, 1], F32)
    mnrow = sb("mnrow", [1, 512], F32)
    rowA = sb("rowA", [1, 512], F32)
    rowB = sb("rowB", [1, 512], F32)
    rrow_bf = sb("rrow_bf", [1, 512], BF16)
    nmr_bf = sb("nmr_bf", [1, 512], BF16)
    rbc = sb("rbc", [128, 512], BF16)
    nbc = sb("nbc", [128, 512], BF16)
    tmb = sb("tmb", [128, 512], BF16)

    ps_proj = [ps(f"ps_proj{i}", [128, 512], F32) for i in range(2)]
    ps_sum = ps("ps_sum", [1, 512], F32)
    ps_sq = ps("ps_sq", [1, 512], F32)

    r = Rec()
    eps_set = r.op("vector", "memset", epst[:, :], 1e-5)
    ldo = r.dma("sync", onesb_s[:, :], onesb, stream="const")

    wlast = [None, None]
    for m in range(16):
        src = wo[:, m * 128:(m + 1) * 128].rearrange("(kc p) j -> p kc j", p=128)
        buf = wstage[m % 2]
        ld = r.dma("sync", buf.rearrange("p (kc j) -> p kc j", kc=16), src,
                   deps=[wlast[m % 2]], stream=f"wst{m % 2}")
        cv = r.op("vector", "tensor_copy",
                  wo_s.rearrange("p (kc m j) -> p kc m j", kc=16, m=16)[:, :, m, :],
                  buf.rearrange("p (kc j) -> p kc j", kc=16), deps=[ld])
        wlast[m % 2] = cv
    wdeps = list(wlast)

    NT2 = TOK // 512
    L_att = Last(1)
    L_ps = Last(2)
    L_stats = Last(1)
    L_sqb = Last(2)
    L_rows = Last(1)
    L_x = Last(2)
    L_y = Last(2)
    L_rbf = Last(1)
    pp = 0
    for ti in range(NT2):
        tsl = slice(ti * 512, (ti + 1) * 512)
        lda = r.dma("sync", att_s.rearrange("p (c t) -> p c t", c=16),
                    att[:, tsl].rearrange("(c p) t -> p c t", p=128),
                    deps=L_att.get(0), stream="att")
        evs = []
        for m in range(16):
            last = None
            for kc in range(16):
                dd = []
                if kc == 0:
                    dd = [lda] + L_ps.get(pp) + (wdeps if ti == 0 else [])
                last = r.op("tensor", "matmul", ps_proj[pp][:, :],
                            wo_s[:, (kc * 16 + m) * 128:(kc * 16 + m) * 128 + 128],
                            att_s[:, kc * 512:(kc + 1) * 512],
                            start=(kc == 0), stop=(kc == 15), deps=dd)
            ev = r.op("vector", "tensor_copy", ao_bf[:, m * 512:(m + 1) * 512],
                      ps_proj[pp][:, :], deps=[last] + L_att.get(0)[:0])
            L_ps.set(pp, ev)
            pp ^= 1
            sq = r.op("scalar", "activation", sqb[m % 2][:, :],
                      ao_bf[:, m * 512:(m + 1) * 512], AF.Square,
                      deps=[ev] + L_sqb.get(m))
            dd = [ev]
            if m == 0:
                dd += [ldo] + L_stats.get(0)
            su = r.op("tensor", "matmul", ps_sum[:, :], onesb_s[:, 0:1],
                      ao_bf[:, m * 512:(m + 1) * 512],
                      start=(m == 0), stop=(m == 15), deps=dd)
            dd2 = [sq]
            if m == 0:
                dd2 += L_stats.get(0)
            sqm = r.op("tensor", "matmul", ps_sq[:, :], onesb_s[:, 0:1], sqb[m % 2][:, :],
                       start=(m == 0), stop=(m == 15), deps=dd2)
            L_sqb.set(m, sqm)
            evs.append(ev)
        sum_last, sq_last = su, sqm
        mn = r.op("scalar", "mul", mnrow[:, :], ps_sum[:, :], -1.0 / 2048.0, deps=[sum_last])
        e2 = r.op("scalar", "mul", rowA[:, :], ps_sq[:, :], 1.0 / 2048.0, deps=[sq_last])
        m2 = r.op("scalar", "activation", rowB[:, :], mnrow[:, :], AF.Square,
                  deps=[mn] + L_rbf.get(0))
        var = r.op("vector", "tensor_sub", rowA[:, :], rowA[:, :], rowB[:, :], deps=[e2, m2])
        sd = r.op("scalar", "activation", rowB[:, :], rowA[:, :], AF.Sqrt, bias=epst[:, :],
                  deps=[var, m2, eps_set])
        rr = r.op("vector", "reciprocal", rowA[:, :], rowB[:, :], deps=[sd, var])
        nm = r.op("vector", "tensor_mul", rowB[:, :], mnrow[:, :], rowA[:, :], deps=[rr, mn, sd])
        L_stats.set(0, mn, e2)
        rb = r.op("scalar", "copy", rrow_bf[:, :], rowA[:, :], deps=[rr, nm])
        nb = r.op("scalar", "copy", nmr_bf[:, :], rowB[:, :], deps=[nm])
        L_rbf.set(0, nb)
        st1 = r.dma("gpsimd", rows_dram[0:1, tsl], rrow_bf[:, :], deps=[rb], stream="rows0")
        st2 = r.dma("gpsimd", rows_dram[1:2, tsl], nmr_bf[:, :], deps=[nb], stream="rows1")
        bc1 = r.dma("gpsimd", rbc[:, :], rows_dram[0:1, tsl].broadcast_to((128, 512)),
                    deps=[st1] + L_rows.get(0), stream="rbc")
        bc2 = r.dma("gpsimd", nbc[:, :], rows_dram[1:2, tsl].broadcast_to((128, 512)),
                    deps=[st2] + L_rows.get(0), stream="nbc")
        ap_ops = []
        for m in range(16):
            ldx = r.dma("sync", xstage[m % 2][:, :], x[m * 128:(m + 1) * 128, tsl],
                        deps=L_x.get(m), stream=f"x{m % 2}")
            t1 = r.op("vector", "tensor_mul", tmb[:, :], ao_bf[:, m * 512:(m + 1) * 512],
                      rbc[:, :], deps=[bc1, evs[m]] + L_att.get(0)[:0])
            t2 = r.op("vector", "tensor_add", tmb[:, :], tmb[:, :], nbc[:, :], deps=[bc2, t1])
            yo = r.op("vector", "tensor_add", y1t[m % 2][:, :], xstage[m % 2][:, :], tmb[:, :],
                      deps=[ldx, t2] + L_y.get(m))
            sty = r.dma("sync", y1[m * 128:(m + 1) * 128, tsl], y1t[m % 2][:, :],
                        deps=[yo], stream="y1out")
            L_x.set(m, yo)
            L_y.set(m, sty)
            ap_ops.append(yo)
        L_att.set(0, ap_ops[-1], ap_ops[-2])

    r.full_wait.update(["const"])
    ssems = open_stream_sems(nc, es, r)
    with nc.Block() as blk:
        r.finalize(blk, sems, ssems)
    es.close()
    return nc


def p2_inputs(full_inputs, p1_outs):
    t = make_tables()
    x = np.asarray(full_inputs["L0_attn_hidden_states"])
    wo = np.asarray(full_inputs["L0_attn_Wo"])
    woT = np.ascontiguousarray(wo.T)
    gseq = [np.concatenate([np.asarray(p1_outs[s * 4 + g]) for g in range(4)], axis=0)
            for s in range(2)]
    maps = []
    for cid in range(NCORES):
        s = cid // 4
        t0 = (cid % 4) * TOK
        maps.append({
            "attn_t": np.ascontiguousarray(gseq[s][:, t0:t0 + TOK]),
            "x_tok": np.ascontiguousarray(fm(x[s])[:, t0:t0 + TOK]),
            "woT": woT, "ones_bf": t["ones_bf"],
        })
    return maps


def ln_rows(r, ctx, tsl, ps_sum, ps_sq, sum_last, sq_last, inv_n, s0, s1):
    mn = r.op("scalar", "mul", ctx["mnrow"][:, :], ps_sum[:, :], -inv_n, deps=[sum_last])
    e2 = r.op("scalar", "mul", ctx["rowA"][:, :], ps_sq[:, :], inv_n, deps=[sq_last])
    m2 = r.op("scalar", "activation", ctx["rowB"][:, :], ctx["mnrow"][:, :], AF.Square,
              deps=[mn] + ctx["L_rbf"].get(0))
    var = r.op("vector", "tensor_sub", ctx["rowA"][:, :], ctx["rowA"][:, :], ctx["rowB"][:, :],
               deps=[e2, m2])
    sd = r.op("scalar", "activation", ctx["rowB"][:, :], ctx["rowA"][:, :], AF.Sqrt,
              bias=ctx["epst"][:, :], deps=[var, m2, ctx["eps_set"]])
    rr = r.op("vector", "reciprocal", ctx["rowA"][:, :], ctx["rowB"][:, :], deps=[sd, var])
    nm = r.op("vector", "tensor_mul", ctx["rowB"][:, :], ctx["mnrow"][:, :], ctx["rowA"][:, :],
              deps=[rr, mn, sd])
    ctx["L_stats"].set(0, mn, e2)
    rb = r.op("scalar", "copy", ctx["rrow_bf"][:, :], ctx["rowA"][:, :], deps=[rr, nm])
    nb = r.op("scalar", "copy", ctx["nmr_bf"][:, :], ctx["rowB"][:, :], deps=[nm])
    ctx["L_rbf"].set(0, nb)
    st1 = r.dma("gpsimd", ctx["rows_dram"][0:1, tsl], ctx["rrow_bf"][:, :], deps=[rb], stream=s0)
    st2 = r.dma("gpsimd", ctx["rows_dram"][1:2, tsl], ctx["nmr_bf"][:, :], deps=[nb], stream=s1)
    bc1 = r.dma("gpsimd", ctx["rbc"][:, :], ctx["rows_dram"][0:1, tsl].broadcast_to((128, 512)),
                deps=[st1] + ctx["L_rows"].get(0), stream=s0 + "b")
    bc2 = r.dma("gpsimd", ctx["nbc"][:, :], ctx["rows_dram"][1:2, tsl].broadcast_to((128, 512)),
                deps=[st2] + ctx["L_rows"].get(0), stream=s1 + "b")
    return bc1, bc2


def mk_ln_ctx(nc, es, r, sbfn):
    ctx = {}
    ctx["epst"] = sbfn("epst", [1, 1], F32)
    ctx["mnrow"] = sbfn("mnrow", [1, 512], F32)
    ctx["rowA"] = sbfn("rowA", [1, 512], F32)
    ctx["rowB"] = sbfn("rowB", [1, 512], F32)
    ctx["rrow_bf"] = sbfn("rrow_bf", [1, 512], BF16)
    ctx["nmr_bf"] = sbfn("nmr_bf", [1, 512], BF16)
    ctx["rbc"] = sbfn("rbc", [128, 512], BF16)
    ctx["nbc"] = sbfn("nbc", [128, 512], BF16)
    ctx["rows_dram"] = nc.dram_tensor("rows_scr", [2, TOK], BF16).ap()
    ctx["L_stats"] = Last(1)
    ctx["L_rows"] = Last(1)
    ctx["L_rbf"] = Last(1)
    ctx["eps_set"] = r.op("vector", "memset", ctx["epst"][:, :], 1e-5)
    return ctx


def build_p3():
    nc = new_nc()
    y1 = nc.dram_tensor("y1", [D_MODEL, TOK], F32, kind="ExternalInput").ap()
    wup = nc.dram_tensor("wupT", [D_MODEL, 8192], F32, kind="ExternalInput").ap()
    wgt = nc.dram_tensor("wgtT", [D_MODEL, 8192], F32, kind="ExternalInput").ap()
    wdn = nc.dram_tensor("wdnT", [8192, D_MODEL], F32, kind="ExternalInput").ap()
    su_ug = nc.dram_tensor("su_ug", [128, 16], F32, kind="ExternalInput").ap()
    su_dp = nc.dram_tensor("su_dp", [128, 64], F32, kind="ExternalInput").ap()
    ws_up = nc.dram_tensor("ws_up", [128, 64], F32, kind="ExternalInput").ap()
    ws_gt = nc.dram_tensor("ws_gt", [128, 64], F32, kind="ExternalInput").ap()
    ws_dp = nc.dram_tensor("ws_dp", [128, 16], F32, kind="ExternalInput").ap()
    h128 = nc.dram_tensor("h128", [128, 128], BF16, kind="ExternalInput").ap()
    onesb = nc.dram_tensor("ones_bf", [128, 128], BF16, kind="ExternalInput").ap()
    y2 = nc.dram_tensor("y2", [D_MODEL, TOK], F32, kind="ExternalOutput").ap()
    wup_bf = nc.dram_tensor("wup_bf", [D_MODEL, 8192], BF16).ap()
    wgt_bf = nc.dram_tensor("wgt_bf", [D_MODEL, 8192], BF16).ap()
    wdn_bf = nc.dram_tensor("wdn_bf", [8192, D_MODEL], BF16).ap()

    es = ExitStack()
    sems = open_sems(nc, es)
    sb = lambda name, shape, dt: es.enter_context(nc.sbuf_tensor(name, shape, dt))
    ps = lambda name, shape, dt: es.enter_context(nc.psum_tensor(name, shape, dt))

    wstage = [sb(f"wst{i}", [128, 1024], F32) for i in range(2)]
    wcv = [sb(f"wcv{i}", [128, 1024], BF16) for i in range(2)]
    wbf = [sb(f"wbf{i}", [128, 16 * 512], BF16) for i in range(2)]
    scr16 = sb("scr16", [128, 16 * 512], BF16)
    xt = sb("xt", [128, 16 * 512], BF16)
    zA = sb("zA", [128, 16 * 512], BF16)
    zB = sb("zB", [128, 16 * 512], BF16)
    A2 = sb("A2", [128, 64 * 512], BF16)
    btmp = {"vector": sb("btmpv", [128, 512], BF16), "gpsimd": sb("btmpg", [128, 512], BF16)}
    sqb = [sb(f"sqb{i}", [128, 512], BF16) for i in range(2)]
    sgb = sb("sgb", [128, 512], BF16)
    onesb_s = sb("onesb_s", [128, 128], BF16)
    h128_s = sb("h128_s", [128, 128], BF16)
    su_ug_s = sb("su_ug_s", [128, 16], F32)
    su_dp_s = sb("su_dp_s", [128, 64], F32)
    ws_up_s = sb("ws_up_s", [128, 64], F32)
    ws_gt_s = sb("ws_gt_s", [128, 64], F32)
    ws_dp_s = sb("ws_dp_s", [128, 16], F32)
    ystage = [sb(f"ys{i}", [128, 512], F32) for i in range(2)]
    y2t = [sb(f"y2t{i}", [128, 512], F32) for i in range(2)]
    tmb = sb("tmb", [128, 512], BF16)

    pj = [ps(f"pj{i}", [128, 512], F32) for i in range(4)]
    ps_sum = ps("ps_sum", [1, 512], F32)
    ps_sq = ps("ps_sq", [1, 512], F32)

    r = Rec()
    ctx = mk_ln_ctx(nc, es, r, sb)
    ld_ops = [
        r.dma("sync", onesb_s[:, :], onesb, stream="const"),
        r.dma("sync", h128_s[:, :], h128, stream="const"),
        r.dma("sync", su_ug_s[:, :], su_ug, stream="const"),
        r.dma("sync", su_dp_s[:, :], su_dp, stream="const"),
        r.dma("sync", ws_up_s[:, :], ws_up, stream="const"),
        r.dma("sync", ws_gt_s[:, :], ws_gt, stream="const"),
        r.dma("sync", ws_dp_s[:, :], ws_dp, stream="const"),
    ]
    sc1 = r.op("vector", "tensor_scalar", ws_up_s[:, :], ws_up_s[:, :],
               1.0 / math.sqrt(2048.0), None, ALU.mult, deps=[ld_ops[4]])
    sc2 = r.op("vector", "tensor_scalar", ws_gt_s[:, :], ws_gt_s[:, :],
               1.0 / math.sqrt(2048.0), None, ALU.mult, deps=[ld_ops[5]])
    sc3 = r.op("vector", "tensor_scalar", ws_dp_s[:, :], ws_dp_s[:, :],
               1.0 / math.sqrt(8192.0), None, ALU.mult, deps=[ld_ops[6]])

    wlast = [None, None]
    pre_done = []
    for (src_t, dst_t, ncols) in ((wup, wup_bf, 8192), (wgt, wgt_bf, 8192), (wdn, wdn_bf, D_MODEL)):
        nrows = 8192 if ncols == D_MODEL else D_MODEL
        nblk = nrows // 128
        for b in range(nblk):
            for cg in range(ncols // 1024):
                sl = (b * (ncols // 1024) + cg) % 2
                ld = r.dma("sync", wstage[sl][:, :],
                           src_t[b * 128:(b + 1) * 128, cg * 1024:(cg + 1) * 1024],
                           deps=[wlast[sl]], stream=f"wst{sl}")
                cv = r.op("vector", "tensor_copy", wcv[sl][:, :], wstage[sl][:, :], deps=[ld])
                st = r.dma("sync", dst_t[b * 128:(b + 1) * 128, cg * 1024:(cg + 1) * 1024],
                           wcv[sl][:, :], deps=[cv], stream=f"wcv{sl}")
                wlast[sl] = st
    pre_done = [x for x in wlast if x is not None]

    NT2 = TOK // 512
    L_y1l = Last(2)
    L_scr = Last(1)
    L_xt = Last(1)
    L_zA = Last(1)
    L_zB = Last(1)
    L_A2 = Last(1)
    L_pj = Last(4)
    L_sqb = Last(2)
    L_wbf = Last(2)
    L_sgb = Last(1)
    L_ys = Last(2)
    L_y2 = Last(2)
    L_tmb = Last(1)
    ppj = 0

    for ti in range(NT2):
        tsl = slice(ti * 512, (ti + 1) * 512)
        cvo = []
        sum_last = sq_last = None
        for c in range(16):
            sl2 = c % 2
            ld = r.dma("sync", wstage[sl2][:, 0:512].bitcast(F32), y1[c * 128:(c + 1) * 128, tsl],
                       deps=([wlast[sl2]] if ti == 0 and c < 2 else []) + L_y1l.get(c), stream=f"wst{sl2}")
            cv = r.op("vector", "tensor_copy", scr16[:, c * 512:(c + 1) * 512],
                      wstage[sl2][:, 0:512], deps=[ld] + (L_scr.get(0) if c == 0 else []))
            L_y1l.set(c, cv)
            cvo.append(cv)
            dd = [cv]
            if c == 0:
                dd += [ld_ops[0]] + ctx["L_stats"].get(0)
            sum_last = r.op("tensor", "matmul", ps_sum[:, :], onesb_s[:, 0:1],
                            scr16[:, c * 512:(c + 1) * 512], start=(c == 0), stop=(c == 15),
                            deps=dd)
            sq = r.op("scalar", "activation", sqb[c % 2][:, :], scr16[:, c * 512:(c + 1) * 512],
                      AF.Square, deps=[cv] + L_sqb.get(c))
            dd2 = [sq]
            if c == 0:
                dd2 += ctx["L_stats"].get(0)
            sq_last = r.op("tensor", "matmul", ps_sq[:, :], onesb_s[:, 0:1], sqb[c % 2][:, :],
                           start=(c == 0), stop=(c == 15), deps=dd2)
            L_sqb.set(c, sq_last)
        bc1, bc2 = ln_rows(r, ctx, tsl, ps_sum, ps_sq, sum_last, sq_last, 1.0 / 2048.0,
                           "rows0", "rows1")
        ap_ops = []
        for c in range(16):
            p1o = r.op("vector", "tensor_mul", xt[:, c * 512:(c + 1) * 512],
                       scr16[:, c * 512:(c + 1) * 512], ctx["rbc"][:, :],
                       deps=[bc1, cvo[c]] + L_xt.get(0))
            p2o = r.op("vector", "tensor_add", xt[:, c * 512:(c + 1) * 512],
                       xt[:, c * 512:(c + 1) * 512], ctx["nbc"][:, :], deps=[bc2, p1o])
            p3o = r.op("vector", "tensor_scalar", xt[:, c * 512:(c + 1) * 512],
                       xt[:, c * 512:(c + 1) * 512], su_ug_s[:, c:c + 1], None, ALU.mult,
                       deps=[p2o, ld_ops[2]])
            ap_ops.append(p3o)
        ctx["L_rows"].set(0, *ap_ops[-2:])
        st1e = []
        for c in range(16):
            mm = r.op("tensor", "matmul", pj[ppj][:, :], h128_s[:, :],
                      xt[:, c * 512:(c + 1) * 512], start=True, stop=True,
                      deps=[ap_ops[c], ld_ops[1]] + L_pj.get(ppj))
            ev = r.op("vector", "tensor_copy", zA[:, c * 512:(c + 1) * 512], pj[ppj][:, :],
                      deps=[mm] + (L_zA.get(0) if c == 0 else []))
            L_pj.set(ppj, ev)
            ppj = (ppj + 1) % 4
            st1e.append(ev)
        L_xt.set(0, *[st1e[-1]])
        cur, nxt = zA, zB
        curdeps = {c: st1e[c] for c in range(16)}
        for lev, b in enumerate((1, 2, 4, 8)):
            newdeps = {}
            warN = L_zB.get(0) if lev == 0 else []
            for c in range(16):
                if c & b:
                    continue
                eng = "vector" if (c // b) % 2 == 0 else "gpsimd"
                d1 = r.op(eng, "tensor_add", nxt[:, c * 512:(c + 1) * 512],
                          cur[:, c * 512:(c + 1) * 512], cur[:, (c + b) * 512:(c + b + 1) * 512],
                          deps=[curdeps[c], curdeps[c + b]] + warN)
                d2 = r.op(eng, "tensor_sub", nxt[:, (c + b) * 512:(c + b + 1) * 512],
                          cur[:, c * 512:(c + 1) * 512], cur[:, (c + b) * 512:(c + b + 1) * 512],
                          deps=[curdeps[c], curdeps[c + b]])
                newdeps[c] = d1
                newdeps[c + b] = d2
            curdeps = newdeps
            cur, nxt = nxt, cur
        zfin = cur
        zdeps = curdeps
        L_zB.set(0, zdeps[15])
        h_deps = {}
        for grp in range(32):
            sl2 = grp % 2
            lw_up = r.dma("sync", wbf[sl2].rearrange("p (kc j) -> p kc j", kc=16)[:, :, 0:256],
                          wup_bf[:, grp * 256:grp * 256 + 256].rearrange("(kc p) j -> p kc j", p=128),
                          deps=pre_done + L_wbf.get(sl2), stream=f"wb{sl2}")
            lw_gt = r.dma("sync", wbf[sl2].rearrange("p (kc j) -> p kc j", kc=16)[:, :, 256:512],
                          wgt_bf[:, grp * 256:grp * 256 + 256].rearrange("(kc p) j -> p kc j", p=128),
                          deps=pre_done, stream=f"wb{sl2}")
            wconsumers = []
            for sub in range(2):
                ic = grp * 2 + sub
                pu = pj[ppj]
                lastu = None
                for kc in range(16):
                    dd = []
                    if kc == 0:
                        dd = [lw_up, zdeps[0]] + list(zdeps.values())[:1] + L_pj.get(ppj)
                        dd += [zdeps[kcx] for kcx in range(16)]
                    lastu = r.op("tensor", "matmul", pu[:, :],
                                 wbf[sl2][:, (kc * 4 + sub) * 128:(kc * 4 + sub) * 128 + 128],
                                 zfin[:, kc * 512:(kc + 1) * 512],
                                 start=(kc == 0), stop=(kc == 15), deps=dd)
                ppu = ppj
                ppj = (ppj + 1) % 4
                pg = pj[ppj]
                lastg = None
                for kc in range(16):
                    dd = []
                    if kc == 0:
                        dd = [lw_gt] + L_pj.get(ppj)
                    lastg = r.op("tensor", "matmul", pg[:, :],
                                 wbf[sl2][:, (kc * 4 + 2 + sub) * 128:(kc * 4 + 2 + sub) * 128 + 128],
                                 zfin[:, kc * 512:(kc + 1) * 512],
                                 start=(kc == 0), stop=(kc == 15), deps=dd)
                ppg = ppj
                ppj = (ppj + 1) % 4
                sg = r.op("scalar", "activation", sgb[:, :], pg[:, :], AF.Silu,
                          scale=ws_gt_s[:, ic:ic + 1], deps=[lastg, sc2] + L_sgb.get(0))
                hh = r.op("vector", "scalar_tensor_tensor", A2[:, ic * 512:(ic + 1) * 512],
                          pu[:, :], ws_up_s[:, ic:ic + 1], sgb[:, :], ALU.mult, ALU.mult,
                          deps=[lastu, sg, sc1] + (L_A2.get(0) if ic == 0 else []))
                h2 = r.op("vector", "tensor_scalar", A2[:, ic * 512:(ic + 1) * 512],
                          A2[:, ic * 512:(ic + 1) * 512], su_dp_s[:, ic:ic + 1], None, ALU.mult,
                          deps=[hh, ld_ops[3]])
                L_pj.set(ppu, hh)
                L_pj.set(ppg, sg)
                L_sgb.set(0, hh)
                h_deps[ic] = h2
                wconsumers.append(lastu)
                wconsumers.append(lastg)
            L_wbf.set(sl2, *wconsumers)
        st2e = {}
        for c in range(64):
            mm = r.op("tensor", "matmul", pj[ppj][:, :], h128_s[:, :],
                      A2[:, c * 512:(c + 1) * 512], start=True, stop=True,
                      deps=[h_deps[c], ld_ops[1]] + L_pj.get(ppj))
            ev = r.op("vector", "tensor_copy", A2[:, c * 512:(c + 1) * 512], pj[ppj][:, :],
                      deps=[mm])
            L_pj.set(ppj, ev)
            ppj = (ppj + 1) % 4
            st2e[c] = ev
        curdeps = st2e
        for lev, b in enumerate((1, 2, 4, 8, 16, 32)):
            newdeps = {}
            for c in range(64):
                if c & b:
                    continue
                eng = "vector" if (c // b) % 2 == 0 else "gpsimd"
                tmp = btmp[eng]
                cp = r.op(eng, "tensor_copy", tmp[:, :], A2[:, c * 512:(c + 1) * 512],
                          deps=[curdeps[c]])
                d1 = r.op(eng, "tensor_add", A2[:, c * 512:(c + 1) * 512],
                          tmp[:, :], A2[:, (c + b) * 512:(c + b + 1) * 512],
                          deps=[cp, curdeps[c + b]])
                d2 = r.op(eng, "tensor_sub", A2[:, (c + b) * 512:(c + b + 1) * 512],
                          tmp[:, :], A2[:, (c + b) * 512:(c + b + 1) * 512],
                          deps=[cp, d1])
                newdeps[c] = d1
                newdeps[c + b] = d2
            curdeps = newdeps
        hfin = A2
        hdeps = curdeps
        sum_last = sq_last = None
        ffevs = []
        for m in range(16):
            sl2 = m % 2
            lw = r.dma("sync", wbf[sl2].rearrange("p (kc j) -> p kc j", kc=64, j=128),
                       wdn_bf[:, m * 128:(m + 1) * 128].rearrange("(kc p) j -> p kc j", p=128),
                       deps=pre_done + L_wbf.get(sl2), stream=f"wb{sl2}")
            pu = pj[ppj]
            last = None
            for kc in range(64):
                dd = []
                if kc == 0:
                    dd = [lw] + L_pj.get(ppj) + [hdeps[i] for i in range(64)]
                last = r.op("tensor", "matmul", pu[:, :],
                            wbf[sl2][:, kc * 128:(kc + 1) * 128],
                            hfin[:, kc * 512:(kc + 1) * 512],
                            start=(kc == 0), stop=(kc == 63), deps=dd)
            L_wbf.set(sl2, last)
            ev = r.op("vector", "tensor_scalar", scr16[:, m * 512:(m + 1) * 512], pu[:, :],
                      ws_dp_s[:, m:m + 1], None, ALU.mult,
                      deps=[last, sc3] + (L_scr.get(0) if m == 0 else []) + (cvo if m == 0 else []))
            L_pj.set(ppj, ev)
            ppj = (ppj + 1) % 4
            ffevs.append(ev)
            sq = r.op("scalar", "activation", sqb[m % 2][:, :], scr16[:, m * 512:(m + 1) * 512],
                      AF.Square, deps=[ev] + L_sqb.get(m))
            dd = [ev]
            if m == 0:
                dd += [ld_ops[0]] + ctx["L_stats"].get(0)
            sum_last = r.op("tensor", "matmul", ps_sum[:, :], onesb_s[:, 0:1],
                            scr16[:, m * 512:(m + 1) * 512], start=(m == 0), stop=(m == 15),
                            deps=dd)
            dd2 = [sq]
            if m == 0:
                dd2 += ctx["L_stats"].get(0)
            sq_last = r.op("tensor", "matmul", ps_sq[:, :], onesb_s[:, 0:1], sqb[m % 2][:, :],
                           start=(m == 0), stop=(m == 15), deps=dd2)
            L_sqb.set(m, sq_last)
        bc1, bc2 = ln_rows(r, ctx, tsl, ps_sum, ps_sq, sum_last, sq_last, 1.0 / 2048.0,
                           "rows0", "rows1")
        scr_cons = []
        for m in range(16):
            ldy = r.dma("sync", ystage[m % 2][:, :], y1[m * 128:(m + 1) * 128, tsl],
                        deps=L_ys.get(m), stream=f"ys{m % 2}")
            t1 = r.op("vector", "tensor_mul", tmb[:, :], scr16[:, m * 512:(m + 1) * 512],
                      ctx["rbc"][:, :], deps=[bc1, ffevs[m]] + L_tmb.get(0))
            t2 = r.op("vector", "tensor_add", tmb[:, :], tmb[:, :], ctx["nbc"][:, :],
                      deps=[bc2, t1])
            yo = r.op("vector", "tensor_add", y2t[m % 2][:, :], ystage[m % 2][:, :], tmb[:, :],
                      deps=[ldy, t2] + L_y2.get(m))
            sty = r.dma("sync", y2[m * 128:(m + 1) * 128, tsl], y2t[m % 2][:, :],
                        deps=[yo], stream="y2out")
            L_ys.set(m, yo)
            L_y2.set(m, sty)
            L_tmb.set(0, yo)
            scr_cons.append(t1)
        ctx["L_rows"].set(0, scr_cons[-1], scr_cons[-2])
        L_scr.set(0, *scr_cons)
        L_zA.set(0, h_deps[63])
        L_A2.set(0, ffevs[-1])

    r.full_wait.update(["const"])
    ssems = open_stream_sems(nc, es, r)
    with nc.Block() as blk:
        r.finalize(blk, sems, ssems)
    es.close()
    return nc


def p3_inputs(full_inputs, y1_slices):
    t = make_tables()
    su_ug = np.asarray(full_inputs["L0_ffn_SU_ug"]).reshape(16, 128).T.copy()
    su_dp = np.asarray(full_inputs["L0_ffn_SU_dp"]).reshape(64, 128).T.copy()
    wsug = np.asarray(full_inputs["L0_ffn_Wscale_ug"])
    ws_up = wsug[:8192].reshape(64, 128).T.copy()
    ws_gt = wsug[8192:].reshape(64, 128).T.copy()
    ws_dp = np.asarray(full_inputs["L0_ffn_Wscale_dp"]).reshape(16, 128).T.copy()
    wupT = np.ascontiguousarray(np.asarray(full_inputs["L0_ffn_W_up"]).T)
    wgtT = np.ascontiguousarray(np.asarray(full_inputs["L0_ffn_W_gate"]).T)
    wdnT = np.ascontiguousarray(np.asarray(full_inputs["L0_ffn_W_down"]).T)
    maps = []
    for cid in range(NCORES):
        maps.append({
            "y1": np.ascontiguousarray(y1_slices[cid], dtype=np.float32),
            "wupT": wupT, "wgtT": wgtT, "wdnT": wdnT,
            "su_ug": su_ug.astype(np.float32), "su_dp": su_dp.astype(np.float32),
            "ws_up": ws_up.astype(np.float32), "ws_gt": ws_gt.astype(np.float32),
            "ws_dp": ws_dp.astype(np.float32),
            "h128": t["h128"], "ones_bf": t["ones_bf"],
        })
    return maps


def build_p4():
    nc = new_nc()
    y2 = nc.dram_tensor("y2", [D_MODEL, TOK], F32, kind="ExternalInput").ap()
    wkv = nc.dram_tensor("wkvT", [D_MODEL, D_MODEL], F32, kind="ExternalInput").ap()
    oh16 = nc.dram_tensor("oh16", [16, 128, 16], BF16, kind="ExternalInput").ap()
    onesb = nc.dram_tensor("ones_bf", [128, 128], BF16, kind="ExternalInput").ap()
    tot = nc.dram_tensor("tot", [128, 16], F32, kind="ExternalOutput").ap()
    rn_dram = nc.dram_tensor("rn_scr", [16, TOK], BF16).ap()

    es = ExitStack()
    sems = open_sems(nc, es)
    sb = lambda name, shape, dt: es.enter_context(nc.sbuf_tensor(name, shape, dt))
    ps = lambda name, shape, dt: es.enter_context(nc.psum_tensor(name, shape, dt))

    wstage = [sb(f"wst{i}", [128, 1024], F32) for i in range(2)]
    wkv_s = sb("wkv_s", [128, 16 * 2048], BF16)
    xt = sb("xt", [128, 16 * 512], BF16)
    scr16 = sb("scr16", [128, 16 * 512], BF16)
    kvb = sb("kvb", [128, 16 * 512], BF16)
    sqb = [sb(f"sqb{i}", [128, 512], BF16) for i in range(2)]
    onesb_s = sb("onesb_s", [128, 128], BF16)
    oh16_s = sb("oh16_s", [128, 16 * 16], BF16)
    n2t = sb("n2t", [16, 512], F32)
    rnt = sb("rnt", [16, 512], BF16)
    rnb = sb("rnb", [128, 16 * 512], BF16)
    redc16 = sb("redc16", [128, 16], F32)
    totacc = sb("totacc", [128, 16], F32)

    pj = [ps(f"pj{i}", [128, 512], F32) for i in range(2)]
    ps_sum = ps("ps_sum", [1, 512], F32)
    ps_sq = ps("ps_sq", [1, 512], F32)
    ps_n2 = ps("ps_n2", [16, 512], F32)

    r = Rec()
    ctx = mk_ln_ctx(nc, es, r, sb)
    ldo = r.dma("sync", onesb_s[:, :], onesb, stream="const")
    ldoh = r.dma("sync", oh16_s.rearrange("p (m n) -> p m n", m=16), oh16.transpose([1, 0, 2]),
                 stream="const")
    tz = r.op("vector", "memset", totacc[:, :], 0.0)

    wlast = [None, None]
    for m in range(16):
        for half in range(2):
            src = wkv[half * 1024:(half + 1) * 1024, m * 128:(m + 1) * 128].rearrange(
                "(kc p) j -> p kc j", p=128)
            buf = wstage[half]
            ld = r.dma("sync", buf.rearrange("p (kc j) -> p kc j", kc=8), src,
                       deps=[wlast[half]], stream=f"wst{half}")
            cv = r.op("vector", "tensor_copy",
                      wkv_s.rearrange("p (kc m j) -> p kc m j", kc=16, m=16)[:, half * 8:half * 8 + 8, m, :],
                      buf.rearrange("p (kc j) -> p kc j", kc=8), deps=[ld])
            wlast[half] = cv
    wdeps = list(wlast)

    NT2 = TOK // 512
    L_y2l = Last(2)
    L_scr = Last(1)
    L_xt = Last(1)
    L_kv = Last(1)
    L_pj = Last(2)
    L_sqb = Last(2)
    L_n2 = Last(1)
    L_rnb = Last(1)
    L_rnt = Last(1)
    ppj = 0
    tot_ops = [tz]
    for ti in range(NT2):
        tsl = slice(ti * 512, (ti + 1) * 512)
        cvo = []
        sum_last = sq_last = None
        for c in range(16):
            sl2 = c % 2
            ld = r.dma("sync", wstage[sl2][:, 0:512], y2[c * 128:(c + 1) * 128, tsl],
                       deps=([wlast[sl2]] if ti == 0 and c < 2 else []) + L_y2l.get(c),
                       stream=f"wst{sl2}")
            cv = r.op("vector", "tensor_copy", scr16[:, c * 512:(c + 1) * 512],
                      wstage[sl2][:, 0:512], deps=[ld] + (L_scr.get(0) if c == 0 else []))
            L_y2l.set(c, cv)
            cvo.append(cv)
            dd = [cv]
            if c == 0:
                dd += [ldo] + ctx["L_stats"].get(0)
            sum_last = r.op("tensor", "matmul", ps_sum[:, :], onesb_s[:, 0:1],
                            scr16[:, c * 512:(c + 1) * 512], start=(c == 0), stop=(c == 15),
                            deps=dd)
            sq = r.op("scalar", "activation", sqb[c % 2][:, :], scr16[:, c * 512:(c + 1) * 512],
                      AF.Square, deps=[cv] + L_sqb.get(c))
            dd2 = [sq]
            if c == 0:
                dd2 += ctx["L_stats"].get(0)
            sq_last = r.op("tensor", "matmul", ps_sq[:, :], onesb_s[:, 0:1], sqb[c % 2][:, :],
                           start=(c == 0), stop=(c == 15), deps=dd2)
            L_sqb.set(c, sq_last)
        bc1, bc2 = ln_rows(r, ctx, tsl, ps_sum, ps_sq, sum_last, sq_last, 1.0 / 2048.0,
                           "rows0", "rows1")
        ap_ops = []
        for c in range(16):
            p1o = r.op("vector", "tensor_mul", xt[:, c * 512:(c + 1) * 512],
                       scr16[:, c * 512:(c + 1) * 512], ctx["rbc"][:, :],
                       deps=[bc1, cvo[c]] + L_xt.get(0))
            p2o = r.op("vector", "tensor_add", xt[:, c * 512:(c + 1) * 512],
                       xt[:, c * 512:(c + 1) * 512], ctx["nbc"][:, :], deps=[bc2, p1o])
            ap_ops.append(p2o)
        ctx["L_rows"].set(0, *ap_ops[-2:])
        L_scr.set(0, *ap_ops)
        kevs = []
        for m in range(16):
            last = None
            for kc in range(16):
                dd = []
                if kc == 0:
                    dd = ap_ops + (wdeps if ti == 0 else []) + L_pj.get(ppj)
                last = r.op("tensor", "matmul", pj[ppj][:, :],
                            wkv_s[:, (kc * 16 + m) * 128:(kc * 16 + m) * 128 + 128],
                            xt[:, kc * 512:(kc + 1) * 512],
                            start=(kc == 0), stop=(kc == 15), deps=dd)
            ev = r.op("vector", "tensor_copy", kvb[:, m * 512:(m + 1) * 512], pj[ppj][:, :],
                      deps=[last] + (L_kv.get(0) if m == 0 else []))
            L_pj.set(ppj, ev)
            ppj ^= 1
            kevs.append(ev)
        L_xt.set(0, last)
        n2_last = None
        for h in range(16):
            sq = r.op("scalar", "activation", sqb[h % 2][:, :], kvb[:, h * 512:(h + 1) * 512],
                      AF.Square, deps=[kevs[h]] + L_sqb.get(h))
            dd = [sq, ldoh]
            if h == 0:
                dd += L_n2.get(0)
            n2_last = r.op("tensor", "matmul", ps_n2[:, :], oh16_s[:, h * 16:(h + 1) * 16],
                           sqb[h % 2][:, :], start=(h == 0), stop=(h == 15), deps=dd)
            L_sqb.set(h, n2_last)
        ev2 = r.op("vector", "tensor_copy", n2t[:, :], ps_n2[:, :], deps=[n2_last])
        L_n2.set(0, ev2)
        sdn = r.op("scalar", "activation", n2t[:, :], n2t[:, :], AF.Sqrt, scale=128.0, deps=[ev2])
        rc = r.op("vector", "reciprocal", n2t[:, :], n2t[:, :], deps=[sdn])
        cb = r.op("scalar", "copy", rnt[:, :], n2t[:, :], deps=[rc] + L_rnt.get(0))
        std = r.dma("gpsimd", rn_dram[:, tsl], rnt[:, :], deps=[cb], stream="rnd")
        bcd = r.dma("gpsimd", rnb.rearrange("p (h t) -> p h t", h=16),
                    rn_dram[:, tsl].unsqueeze(0).broadcast_to((128, 16, 512)),
                    deps=[std] + L_rnb.get(0), stream="rnb")
        L_rnt.set(0, std)
        rss = []
        for h in range(16):
            kn = r.op("vector", "tensor_mul", kvb[:, h * 512:(h + 1) * 512],
                      kvb[:, h * 512:(h + 1) * 512], rnb[:, h * 512:(h + 1) * 512],
                      deps=[bcd, kevs[h]])
            rs = r.op("vector", "reduce_sum", redc16[:, h:h + 1].rearrange("p (a b) -> p a b", b=1),
                      kvb[:, h * 512:(h + 1) * 512].rearrange("p (a b) -> p a b", b=512),
                      axis=AX.X, deps=[kn] + (tot_ops[-1:] if h == 0 else []))
            rss.append(rs)
        ta = r.op("gpsimd", "tensor_add", totacc[:, :], totacc[:, :], redc16[:, :],
                  deps=rss + tot_ops[-1:])
        tot_ops.append(ta)
        L_kv.set(0, ta)
        L_rnb.set(0, ta)

    stt = r.dma("sync", tot, totacc[:, :], deps=[tot_ops[-1]], stream="outs")
    dbg_kv = nc.dram_tensor("dbg_kv", [128, 16 * 512], BF16, kind="ExternalOutput").ap()
    dbg_rn = nc.dram_tensor("dbg_rn", [16, 512], F32, kind="ExternalOutput").ap()
    dbg_rnb = nc.dram_tensor("dbg_rnb", [128, 16 * 512], BF16, kind="ExternalOutput").ap()
    r.dma("sync", dbg_kv, kvb[:, :], deps=[tot_ops[-1]], stream="outs")
    r.dma("sync", dbg_rn, n2t[:, :], deps=[tot_ops[-1]], stream="outs")
    r.dma("sync", dbg_rnb, rnb[:, :], deps=[tot_ops[-1]], stream="outs")
    r.full_wait.update(["const"])
    ssems = open_stream_sems(nc, es, r)
    with nc.Block() as blk:
        r.finalize(blk, sems, ssems)
    es.close()
    return nc


def p4_inputs(full_inputs, y2_slices):
    t = make_tables()
    wkvT = np.ascontiguousarray(np.asarray(full_inputs["L1_attn_W_kv"]).T)
    maps = []
    for cid in range(NCORES):
        maps.append({
            "y2": np.ascontiguousarray(y2_slices[cid], dtype=np.float32),
            "wkvT": wkvT, "oh16": t["oh16"], "ones_bf": t["ones_bf"],
        })
    return maps


def build_p5(dbg=False):
    nc = new_nc()
    y2 = nc.dram_tensor("y2", [D_MODEL, TOK], F32, kind="ExternalInput").ap()
    wq = nc.dram_tensor("wqT", [D_MODEL, D_MODEL], F32, kind="ExternalInput").ap()
    wkv = nc.dram_tensor("wkvT", [D_MODEL, D_MODEL], F32, kind="ExternalInput").ap()
    wo = nc.dram_tensor("woT", [D_MODEL, D_MODEL], F32, kind="ExternalInput").ap()
    tots = nc.dram_tensor("tots", [4, 128, 16], F32, kind="ExternalInput").ap()
    selbc = nc.dram_tensor("selbc", [4, 128, 1], F32, kind="ExternalInput").ap()
    oh16 = nc.dram_tensor("oh16", [16, 128, 16], BF16, kind="ExternalInput").ap()
    onesb = nc.dram_tensor("ones_bf", [128, 128], BF16, kind="ExternalInput").ap()
    dftC = nc.dram_tensor("dftC", [128, 128], BF16, kind="ExternalInput").ap()
    dftS = nc.dram_tensor("dftSneg", [128, 128], BF16, kind="ExternalInput").ap()
    y3 = nc.dram_tensor("y3", [D_MODEL, TOK], F32, kind="ExternalOutput").ap()
    rn_dram = nc.dram_tensor("rn_scr", [16, TOK], BF16).ap()
    rq_dram = nc.dram_tensor("rq_scr", [16, TOK], BF16).ap()

    es = ExitStack()
    sems = open_sems(nc, es)
    sb = lambda name, shape, dt: es.enter_context(nc.sbuf_tensor(name, shape, dt))
    ps = lambda name, shape, dt: es.enter_context(nc.psum_tensor(name, shape, dt))

    HT = 1024
    wstage = [sb(f"wst{i}", [128, 512], F32) for i in range(2)]
    wXs = sb("wXs", [128, 16 * 4 * 128], BF16)
    xt = sb("xt", [128, 16 * HT], BF16)
    kvb = sb("kvb", [128, 16 * HT], BF16)
    Kc = sb("Kc", [128, 16 * HT], BF16)
    qb = sb("qb", [128, 16 * HT], BF16)
    scr = sb("scr", [128, 16 * 512], BF16)
    onestile = sb("onestile", [128, HT], BF16)
    sqb = [sb(f"sqb{i}", [128, 512], BF16) for i in range(2)]
    onesb_s = sb("onesb_s", [128, 128], BF16)
    oh16_s = sb("oh16_s", [128, 16 * 16], BF16)
    dftC_s = sb("dftC_s", [128, 128], BF16)
    dftS_s = sb("dftS_s", [128, 128], BF16)
    n2t = sb("n2t", [16, 512], F32)
    rnt = sb("rnt", [16, 512], BF16)
    rnb = scr
    carry = sb("carry", [128, 16], F32)
    totst = sb("totst", [128, 16], F32)
    selt = sb("selt", [128, 1], F32)
    kre = sb("kre", [128, 512], BF16)
    kim = sb("kim", [128, 512], BF16)
    tt1 = sb("tt1", [128, 512], BF16)
    qvre = sb("qvre", [128, 512], BF16)
    qvim = sb("qvim", [128, 512], BF16)
    ystage = [sb("ys0", [128, 512], F32)] * 2
    y3t = [sb("y3t0", [128, 512], F32)] * 2
    tmb = sb("tmb", [128, 512], BF16)

    pA = ps("pA", [128, 512], F32)
    pB = ps("pB", [128, 512], F32)
    pC = ps("pC", [128, 512], F32)
    pD = ps("pD", [128, 512], F32)
    pE = ps("pE", [128, 512], F32)
    ps_sum = ps("ps_sum", [1, 512], F32)
    ps_sq = ps("ps_sq", [1, 512], F32)

    r = Rec()
    ctx = mk_ln_ctx(nc, es, r, sb)
    ldo = r.dma("sync", onesb_s[:, :], onesb, stream="const")
    ldoh = r.dma("sync", oh16_s.rearrange("p (m n) -> p m n", m=16), oh16.transpose([1, 0, 2]),
                 stream="const")
    ldc = r.dma("sync", dftC_s[:, :], dftC, stream="const")
    lds = r.dma("sync", dftS_s[:, :], dftS, stream="const")
    ones_set = r.op("vector", "memset", onestile[:, :], 1.0)

    cinit = r.op("vector", "memset", carry[:, :], 0.0)
    for j in range(4):
        lt = r.dma("sync", totst[:, :], tots[j], deps=[cinit], stream="tots")
        lsl = r.dma("sync", selt[:, :], selbc[j], deps=[cinit], stream="sel")
        cinit = r.op("vector", "scalar_tensor_tensor", carry[:, :], totst[:, :], selt[:, :],
                     carry[:, :], ALU.mult, ALU.add, deps=[lt, lsl, cinit])
    carry_op = cinit

    def load_w(wap, g2, deps):
        cvs = []
        for mm in range(4):
            m = g2 * 4 + mm
            for qtr in range(4):
                sl2 = qtr % 2
                src = wap[qtr * 512:(qtr + 1) * 512, m * 128:(m + 1) * 128].rearrange(
                    "(kc p) j -> p kc j", p=128)
                ld = r.dma("sync", wstage[sl2].rearrange("p (kc j) -> p kc j", kc=4), src,
                           deps=list(deps) + L_wst.get(sl2), stream=f"wst{sl2}")
                cv = r.op("vector", "tensor_copy",
                          wXs.rearrange("p (kc m j) -> p kc m j", kc=16, m=4)[:, qtr * 4:qtr * 4 + 4, mm, :],
                          wstage[sl2].rearrange("p (kc j) -> p kc j", kc=4), deps=[ld])
                L_wst.set(sl2, cv)
                cvs.append(cv)
        return cvs

    NT2 = TOK // 512
    L_wst = Last(2)
    L_scr = Last(1)
    L_xt = Last(1)
    L_kv = Last(1)
    L_qb = Last(1)
    L_Kc = Last(1)
    L_pAB = Last(2)
    L_pCD = Last(2)
    L_pE = Last(1)
    L_sqb = Last(2)
    L_n2 = Last(1)
    L_rnb = Last(1)
    L_rnt = Last(1)
    L_dfte = Last(1)
    L_wXs = Last(1)
    L_ys = Last(2)
    L_y3 = Last(2)
    L_tmb = Last(1)
    pAB = [pA, pB]
    ppj = 0
    wst_last = [None, None]

    for half in range(2):
        hsl = slice(half * HT, (half + 1) * HT)
        ap_all = {}
        for tc in range(2):
            ti = half * 2 + tc
            tsl = slice(ti * 512, (ti + 1) * 512)
            cvo = []
            sum_last = sq_last = None
            for c in range(16):
                sl2 = c % 2
                ld = r.dma("sync", wstage[sl2][:, :], y2[c * 128:(c + 1) * 128, tsl],
                           deps=L_wst.get(sl2), stream=f"wst{sl2}")
                cv = r.op("vector", "tensor_copy", scr[:, c * 512:(c + 1) * 512],
                          wstage[sl2][:, :],
                          deps=[ld] + ((L_scr.get(0) + L_rnb.get(0)) if c == 0 else []))
                L_wst.set(sl2, cv)
                cvo.append(cv)
                dd = [cv]
                if c == 0:
                    dd += [ldo] + ctx["L_stats"].get(0)
                sum_last = r.op("tensor", "matmul", ps_sum[:, :], onesb_s[:, 0:1],
                                scr[:, c * 512:(c + 1) * 512], start=(c == 0), stop=(c == 15),
                                deps=dd)
                sq = r.op("scalar", "activation", sqb[c % 2][:, :], scr[:, c * 512:(c + 1) * 512],
                          AF.Square, deps=[cv] + L_sqb.get(c))
                dd2 = [sq]
                if c == 0:
                    dd2 += ctx["L_stats"].get(0)
                sq_last = r.op("tensor", "matmul", ps_sq[:, :], onesb_s[:, 0:1], sqb[c % 2][:, :],
                               start=(c == 0), stop=(c == 15), deps=dd2)
                L_sqb.set(c, sq_last)
            bc1, bc2 = ln_rows(r, ctx, tsl, ps_sum, ps_sq, sum_last, sq_last, 1.0 / 2048.0,
                               "rows0", "rows1")
            aps = []
            for c in range(16):
                p1o = r.op("vector", "tensor_mul", xt[:, c * HT + tc * 512: c * HT + tc * 512 + 512],
                           scr[:, c * 512:(c + 1) * 512], ctx["rbc"][:, :],
                           deps=[bc1, cvo[c]] + (L_xt.get(0) if tc == 0 and c == 0 else []))
                p2o = r.op("vector", "tensor_add", xt[:, c * HT + tc * 512: c * HT + tc * 512 + 512],
                           xt[:, c * HT + tc * 512: c * HT + tc * 512 + 512], ctx["nbc"][:, :],
                           deps=[bc2, p1o])
                aps.append(p2o)
            ap_all[tc] = aps
            ctx["L_rows"].set(0, *aps[-2:])
            L_scr.set(0, *aps)
        ap_ops = ap_all[0] + ap_all[1]

        kevs = []
        for g2 in range(4):
            wcv = load_w(wkv, g2, L_wXs.get(0))
            for mm in range(4):
                m = g2 * 4 + mm
                last = None
                for kc in range(16):
                    dd = []
                    if kc == 0:
                        dd = ap_ops + wcv + L_pAB.get(ppj)
                    last = r.op("tensor", "matmul", pAB[ppj][:, 0:512],
                                wXs[:, (kc * 4 + mm) * 128:(kc * 4 + mm) * 128 + 128],
                                xt[:, kc * HT:kc * HT + 512],
                                start=(kc == 0), stop=(kc == 15), deps=dd)
                    last2 = r.op("tensor", "matmul", pAB[ppj ^ 1][:, 0:512],
                                 wXs[:, (kc * 4 + mm) * 128:(kc * 4 + mm) * 128 + 128],
                                 xt[:, kc * HT + 512:kc * HT + 1024],
                                 start=(kc == 0), stop=(kc == 15),
                                 deps=[] if kc else (dd + L_pAB.get(ppj ^ 1)))
                ev1 = r.op("vector", "tensor_copy", kvb[:, m * HT:m * HT + 512], pAB[ppj][:, 0:512],
                           deps=[last] + (L_kv.get(0) if m == 0 else []))
                ev2 = r.op("vector", "tensor_copy", kvb[:, m * HT + 512:m * HT + 1024],
                           pAB[ppj ^ 1][:, 0:512], deps=[last2])
                L_pAB.set(ppj, ev1)
                L_pAB.set(ppj ^ 1, ev2)
                kevs.extend([ev1, ev2])
            L_wXs.set(0, last, last2)
        kn_ops = {}
        for tc in range(2):
            ti = half * 2 + tc
            tsl = slice(ti * 512, (ti + 1) * 512)
            n2_last = None
            for h in range(16):
                sq = r.op("scalar", "activation", sqb[h % 2][:, :],
                          kvb[:, h * HT + tc * 512:h * HT + tc * 512 + 512],
                          AF.Square, deps=[kevs[2 * h + tc]] + L_sqb.get(h))
                dd = [sq, ldoh]
                if h == 0:
                    dd += L_n2.get(0) + L_pCD.get(0)
                n2_last = r.op("tensor", "matmul", pC[0:16, 0:512], oh16_s[:, h * 16:(h + 1) * 16],
                               sqb[h % 2][:, :], start=(h == 0), stop=(h == 15), deps=dd)
                L_sqb.set(h, n2_last)
            ev2 = r.op("vector", "tensor_copy", n2t[:, :], pC[0:16, 0:512], deps=[n2_last])
            L_n2.set(0, ev2)
            L_pCD.set(0, ev2)
            sdn = r.op("scalar", "activation", n2t[:, :], n2t[:, :], AF.Sqrt, scale=128.0,
                       deps=[ev2])
            rc = r.op("vector", "reciprocal", n2t[:, :], n2t[:, :], deps=[sdn])
            cb = r.op("scalar", "copy", rnt[:, :], n2t[:, :], deps=[rc] + L_rnt.get(0))
            std = r.dma("gpsimd", rn_dram[:, tsl], rnt[:, :], deps=[cb], stream="rnd")
            bcd = r.dma("gpsimd", rnb.rearrange("p (h t) -> p h t", h=16),
                        rn_dram[:, tsl].unsqueeze(0).broadcast_to((128, 16, 512)),
                        deps=[std] + L_rnb.get(0) + L_scr.get(0), stream="rnb")
            L_rnt.set(0, std)
            kns = []
            for h in range(16):
                kn = r.op("vector", "tensor_mul", kvb[:, h * HT + tc * 512:h * HT + tc * 512 + 512],
                          kvb[:, h * HT + tc * 512:h * HT + tc * 512 + 512],
                          rnb[:, h * 512:(h + 1) * 512], deps=[bcd, kevs[2 * h + tc]])
                kn_ops[(h, tc)] = kn
                kns.append(kn)
            L_rnb.set(0, *kns[-2:])
        scans = []
        for c in range(16):
            sc = r.op("vector", "tensor_tensor_scan", Kc[:, c * HT:(c + 1) * HT],
                      onestile[:, :], kvb[:, c * HT:(c + 1) * HT], carry[:, c:c + 1],
                      ALU.mult, ALU.add,
                      deps=[ones_set, kn_ops[(c, 0)], kn_ops[(c, 1)], carry_op] + (L_Kc.get(0) if c == 0 else []))
            scans.append(sc)
        carry_op = r.op("gpsimd", "tensor_copy",
                        carry.rearrange("p (c o) -> p c o", c=16),
                        Kc.rearrange("p (c t) -> p c t", c=16)[:, :, HT - 1:HT],
                        deps=scans + [carry_op])
        if dbg and half == 1:
            dbgkv = nc.dram_tensor("dbgkv", [128, 16 * HT], BF16, kind="ExternalOutput").ap()
            r.dma("sync", dbgkv, kvb[:, :], deps=scans + list(kn_ops.values()), stream="dbg")
        qevs = []
        for g2 in range(4):
            wcv = load_w(wq, g2, L_wXs.get(0))
            for mm in range(4):
                m = g2 * 4 + mm
                last = None
                for kc in range(16):
                    dd = []
                    if kc == 0:
                        dd = wcv + L_pAB.get(ppj)
                    last = r.op("tensor", "matmul", pAB[ppj][:, 0:512],
                                wXs[:, (kc * 4 + mm) * 128:(kc * 4 + mm) * 128 + 128],
                                xt[:, kc * HT:kc * HT + 512],
                                start=(kc == 0), stop=(kc == 15), deps=dd)
                    last2 = r.op("tensor", "matmul", pAB[ppj ^ 1][:, 0:512],
                                 wXs[:, (kc * 4 + mm) * 128:(kc * 4 + mm) * 128 + 128],
                                 xt[:, kc * HT + 512:kc * HT + 1024],
                                 start=(kc == 0), stop=(kc == 15),
                                 deps=[] if kc else (dd + L_pAB.get(ppj ^ 1)))
                ev1 = r.op("vector", "tensor_copy", qb[:, m * HT:m * HT + 512], pAB[ppj][:, 0:512],
                           deps=[last] + (L_qb.get(0) if m == 0 else []))
                ev2 = r.op("vector", "tensor_copy", qb[:, m * HT + 512:m * HT + 1024],
                           pAB[ppj ^ 1][:, 0:512], deps=[last2])
                L_pAB.set(ppj, ev1)
                L_pAB.set(ppj ^ 1, ev2)
                qevs.extend([ev1, ev2])
            L_wXs.set(0, last, last2)
        rq_bcast = {}
        for tc in range(2):
            ti = half * 2 + tc
            tsl = slice(ti * 512, (ti + 1) * 512)
            n2_last = None
            for h in range(16):
                sq = r.op("scalar", "activation", sqb[h % 2][:, :],
                          qb[:, h * HT + tc * 512:h * HT + tc * 512 + 512],
                          AF.Square, deps=[qevs[2 * h + tc]] + L_sqb.get(h))
                dd = [sq, ldoh]
                if h == 0:
                    dd += L_n2.get(0) + L_pCD.get(0)
                n2_last = r.op("tensor", "matmul", pC[0:16, 0:512], oh16_s[:, h * 16:(h + 1) * 16],
                               sqb[h % 2][:, :], start=(h == 0), stop=(h == 15), deps=dd)
                L_sqb.set(h, n2_last)
            ev2 = r.op("vector", "tensor_copy", n2t[:, :], pC[0:16, 0:512], deps=[n2_last])
            L_n2.set(0, ev2)
            L_pCD.set(0, ev2)
            sdn = r.op("scalar", "activation", n2t[:, :], n2t[:, :], AF.Sqrt, scale=2097152.0,
                       deps=[ev2])
            rc = r.op("vector", "reciprocal", n2t[:, :], n2t[:, :], deps=[sdn])
            cb = r.op("scalar", "copy", rnt[:, :], n2t[:, :], deps=[rc] + L_rnt.get(0))
            std = r.dma("gpsimd", rq_dram[:, tsl], rnt[:, :], deps=[cb], stream="rqd")
            L_rnt.set(0, std)
            rq_bcast[tc] = std
        vh_ops = {}
        for tc in range(2):
            ti = half * 2 + tc
            tsl = slice(ti * 512, (ti + 1) * 512)
            bcd = r.dma("gpsimd", rnb.rearrange("p (h t) -> p h t", h=16),
                        rq_dram[:, tsl].unsqueeze(0).broadcast_to((128, 16, 512)),
                        deps=[rq_bcast[tc]] + L_rnb.get(0) + L_scr.get(0), stream="rnb")
            vhs = []
            for h in range(16):
                qsl = qb[:, h * HT + tc * 512:h * HT + tc * 512 + 512]
                ksl = Kc[:, h * HT + tc * 512:h * HT + tc * 512 + 512]
                m1 = r.op("tensor", "matmul", pA[:, 0:512], dftC_s[:, :], qsl,
                          start=True, stop=True, deps=[ldc, qevs[2 * h + tc]] + L_pAB.get(0))
                m2 = r.op("tensor", "matmul", pB[:, 0:512], dftS_s[:, :], qsl,
                          start=True, stop=True, deps=[lds] + L_pAB.get(1))
                m3 = r.op("tensor", "matmul", pC[:, 0:512], dftC_s[:, :], ksl,
                          start=True, stop=True, deps=[scans[h], carry_op] + L_pCD.get(0))
                m4 = r.op("tensor", "matmul", pD[:, 0:512], dftS_s[:, :], ksl,
                          start=True, stop=True, deps=[] + L_pCD.get(1))
                e3 = r.op("vector", "tensor_copy", kre[:, :], pC[:, 0:512], deps=[m3])
                e4 = r.op("vector", "tensor_copy", kim[:, :], pD[:, 0:512], deps=[m4])
                L_pCD.set(0, e3)
                L_pCD.set(1, e4)
                t1 = r.op("vector", "tensor_mul", qvre[:, :], pA[:, 0:512], kre[:, :], deps=[m1, e3])
                t2 = r.op("vector", "tensor_mul", tt1[:, :], pB[:, 0:512], kim[:, :], deps=[m2, e4])
                c1 = r.op("vector", "tensor_sub", qvre[:, :], qvre[:, :], tt1[:, :], deps=[t1, t2])
                t3 = r.op("vector", "tensor_mul", qvim[:, :], pA[:, 0:512], kim[:, :], deps=[m1, e4])
                t4 = r.op("vector", "tensor_mul", tt1[:, :], pB[:, 0:512], kre[:, :], deps=[t2, e3, m2])
                c2 = r.op("vector", "tensor_add", qvim[:, :], qvim[:, :], tt1[:, :], deps=[t3, t4])
                L_pAB.set(0, t3)
                L_pAB.set(1, t4)
                m5 = r.op("tensor", "matmul", pE[:, 0:512], dftC_s[:, :], qvre[:, :],
                          start=True, stop=False, deps=[c1] + L_pE.get(0))
                m6 = r.op("tensor", "matmul", pE[:, 0:512], dftS_s[:, :], qvim[:, :],
                          start=False, stop=True, deps=[c2])
                vh = r.op("vector", "tensor_mul", kvb[:, h * HT + tc * 512:h * HT + tc * 512 + 512],
                          pE[:, 0:512], rnb[:, h * 512:(h + 1) * 512],
                          deps=[m6, bcd, kn_ops[(h, tc)], scans[h] if h == 15 else None])
                L_pE.set(0, vh)
                vh_ops[(h, tc)] = vh
                vhs.append(vh)
            L_rnb.set(0, *vhs[-2:])
        L_Kc.set(0, vh_ops[(15, 0)], vh_ops[(15, 1)])

        for tc in range(2):
            ti = half * 2 + tc
            tsl = slice(ti * 512, (ti + 1) * 512)
            sum_last = sq_last = None
            aevs = []
            for g2 in range(4):
                wcv = load_w(wo, g2, L_wXs.get(0))
                for mm in range(4):
                    m = g2 * 4 + mm
                    last = None
                    for kc in range(16):
                        dd = []
                        if kc == 0:
                            dd = wcv + L_pAB.get(ppj) + [vh_ops[(kk, tc)] for kk in range(16)]
                        last = r.op("tensor", "matmul", pAB[ppj][:, 0:512],
                                    wXs[:, (kc * 4 + mm) * 128:(kc * 4 + mm) * 128 + 128],
                                    kvb[:, kc * HT + tc * 512:kc * HT + tc * 512 + 512],
                                    start=(kc == 0), stop=(kc == 15), deps=dd)
                    ev = r.op("vector", "tensor_copy", scr[:, m * 512:(m + 1) * 512],
                              pAB[ppj][:, 0:512],
                              deps=[last] + ((L_scr.get(0) + L_rnb.get(0)) if m == 0 and g2 == 0 else []))
                    L_pAB.set(ppj, ev)
                    ppj ^= 1
                    aevs.append(ev)
                    sq = r.op("scalar", "activation", sqb[m % 2][:, :], scr[:, m * 512:(m + 1) * 512],
                              AF.Square, deps=[ev] + L_sqb.get(m))
                    dd = [ev]
                    if m == 0:
                        dd += [ldo] + ctx["L_stats"].get(0)
                    sum_last = r.op("tensor", "matmul", ps_sum[:, :], onesb_s[:, 0:1],
                                    scr[:, m * 512:(m + 1) * 512], start=(m == 0), stop=(m == 15),
                                    deps=dd)
                    dd2 = [sq]
                    if m == 0:
                        dd2 += ctx["L_stats"].get(0)
                    sq_last = r.op("tensor", "matmul", ps_sq[:, :], onesb_s[:, 0:1], sqb[m % 2][:, :],
                                   start=(m == 0), stop=(m == 15), deps=dd2)
                    L_sqb.set(m, sq_last)
                L_wXs.set(0, last)
            bc1, bc2 = ln_rows(r, ctx, tsl, ps_sum, ps_sq, sum_last, sq_last, 1.0 / 2048.0,
                               "rows0", "rows1")
            scr_cons = []
            for m in range(16):
                ldy = r.dma("sync", ystage[m % 2][:, :], y2[m * 128:(m + 1) * 128, tsl],
                            deps=L_ys.get(m), stream=f"ys{m % 2}")
                t1 = r.op("vector", "tensor_mul", tmb[:, :], scr[:, m * 512:(m + 1) * 512],
                          ctx["rbc"][:, :], deps=[bc1, aevs[m]] + L_tmb.get(0))
                t2 = r.op("vector", "tensor_add", tmb[:, :], tmb[:, :], ctx["nbc"][:, :],
                          deps=[bc2, t1])
                yo = r.op("vector", "tensor_add", y3t[m % 2][:, :], ystage[m % 2][:, :], tmb[:, :],
                          deps=[ldy, t2] + L_y3.get(m))
                sty = r.dma("sync", y3[m * 128:(m + 1) * 128, tsl], y3t[m % 2][:, :],
                            deps=[yo], stream="y3out")
                L_ys.set(m, yo)
                L_y3.set(m, sty)
                L_tmb.set(0, yo)
                scr_cons.append(t1)
            ctx["L_rows"].set(0, scr_cons[-1], scr_cons[-2])
            L_scr.set(0, *scr_cons)
        L_xt.set(0, qevs[-1])
        L_kv.set(0, aevs[-1])
        L_qb.set(0, aevs[-1])

    r.full_wait.update(["const"])
    ssems = open_stream_sems(nc, es, r)
    with nc.Block() as blk:
        r.finalize(blk, sems, ssems)
    es.close()
    return nc


def p5_inputs(full_inputs, y2_slices, tot_list):
    t = make_tables()
    wqT = np.ascontiguousarray(np.asarray(full_inputs["L1_attn_W_q"]).T)
    wkvT = np.ascontiguousarray(np.asarray(full_inputs["L1_attn_W_kv"]).T)
    woT = np.ascontiguousarray(np.asarray(full_inputs["L1_attn_W_o"]).T)
    maps = []
    for cid in range(NCORES):
        s, rr = cid // 4, cid % 4
        tots = np.stack([np.asarray(tot_list[s * 4 + j]) for j in range(4)], axis=0)
        sel = np.zeros((4, 128, 1), np.float32)
        for j in range(rr):
            sel[j] = 1.0
        maps.append({
            "y2": np.ascontiguousarray(y2_slices[cid], dtype=np.float32),
            "wqT": wqT, "wkvT": wkvT, "woT": woT,
            "tots": tots.astype(np.float32), "selbc": sel,
            "oh16": t["oh16"], "ones_bf": t["ones_bf"],
            "dftC": t["dftC"], "dftSneg": t["dftSneg"],
        })
    return maps


def build_p6():
    nc = new_nc()
    y3 = nc.dram_tensor("y3", [D_MODEL, TOK], F32, kind="ExternalInput").ap()
    wg = nc.dram_tensor("wgT", [D_MODEL, 8192], I32, kind="ExternalInput").ap()
    wu = nc.dram_tensor("wuT", [D_MODEL, 8192], I32, kind="ExternalInput").ap()
    wd = nc.dram_tensor("wdT", [8192, D_MODEL], I32, kind="ExternalInput").ap()
    gsc = nc.dram_tensor("gsc", [128, 64], F32, kind="ExternalInput").ap()
    usc = nc.dram_tensor("usc", [128, 64], F32, kind="ExternalInput").ap()
    dsc = nc.dram_tensor("dsc", [128, 16], F32, kind="ExternalInput").ap()
    onesb = nc.dram_tensor("ones_bf", [128, 128], BF16, kind="ExternalInput").ap()
    y4 = nc.dram_tensor("y4", [D_MODEL, TOK], F32, kind="ExternalOutput").ap()
    wg_bf = nc.dram_tensor("wg_bf", [D_MODEL, 8192], BF16).ap()
    wu_bf = nc.dram_tensor("wu_bf", [D_MODEL, 8192], BF16).ap()
    wd_bf = nc.dram_tensor("wd_bf", [8192, D_MODEL], BF16).ap()
    srow_dram = nc.dram_tensor("srow_scr", [2, TOK], BF16).ap()

    es = ExitStack()
    sems = open_sems(nc, es)
    sb = lambda name, shape, dt: es.enter_context(nc.sbuf_tensor(name, shape, dt))
    ps = lambda name, shape, dt: es.enter_context(nc.psum_tensor(name, shape, dt))

    wstage = [sb(f"wst{i}", [128, 1024], I32) for i in range(2)]
    wcv = [sb(f"wcv{i}", [128, 1024], BF16) for i in range(2)]
    wbf = [sb(f"wbf{i}", [128, 16 * 512], BF16) for i in range(2)]
    scr16 = sb("scr16", [128, 16 * 512], BF16)
    xt = sb("xt", [128, 16 * 512], BF16)
    A2 = sb("A2", [128, 64 * 512], BF16)
    sqb = [sb(f"sqb{i}", [128, 512], BF16) for i in range(2)]
    sgb = sb("sgb", [128, 512], BF16)
    tgu = sb("tgu", [128, 512], F32)
    onesb_s = sb("onesb_s", [128, 128], BF16)
    gsc_s = sb("gsc_s", [128, 64], F32)
    usc_s = sb("usc_s", [128, 64], F32)
    dsc_s = sb("dsc_s", [128, 16], F32)
    srow = sb("srow", [1, 512], F32)
    srow_bf = sb("srow_bf", [1, 512], BF16)
    sbc = sb("sbc", [128, 512], BF16)
    shbc = sb("shbc", [128, 512], BF16)
    ystage = [sb("ys0", [128, 512], F32)] * 2
    y4t = [sb("y4t0", [128, 512], F32)] * 2
    tmb = sb("tmb", [128, 512], BF16)

    pj = [ps(f"pj{i}", [128, 512], F32) for i in range(4)]
    ps_sum = ps("ps_sum", [1, 512], F32)
    ps_sq = ps("ps_sq", [1, 512], F32)
    ps_s = ps("ps_s", [1, 512], F32)
    ps_sh = ps("ps_sh", [1, 512], F32)

    r = Rec()
    ctx = mk_ln_ctx(nc, es, r, sb)
    ld_ops = [
        r.dma("sync", onesb_s[:, :], onesb, stream="const"),
        r.dma("sync", gsc_s[:, :], gsc, stream="const"),
        r.dma("sync", usc_s[:, :], usc, stream="const"),
        r.dma("sync", dsc_s[:, :], dsc, stream="const"),
    ]

    wlast = [None, None]
    for (src_t, dst_t, ncols) in ((wg, wg_bf, 8192), (wu, wu_bf, 8192), (wd, wd_bf, D_MODEL)):
        nrows = 8192 if ncols == D_MODEL else D_MODEL
        nblk = nrows // 128
        for b in range(nblk):
            for cg in range(ncols // 1024):
                sl = (b * (ncols // 1024) + cg) % 2
                ld = r.dma("sync", wstage[sl][:, :],
                           src_t[b * 128:(b + 1) * 128, cg * 1024:(cg + 1) * 1024],
                           deps=[wlast[sl]], stream=f"wst{sl}")
                cv = r.op("vector", "tensor_copy", wcv[sl][:, :], wstage[sl][:, :], deps=[ld])
                st = r.dma("sync", dst_t[b * 128:(b + 1) * 128, cg * 1024:(cg + 1) * 1024],
                           wcv[sl][:, :], deps=[cv], stream=f"wcv{sl}")
                wlast[sl] = st
    pre_done = [x for x in wlast if x is not None]

    NT2 = TOK // 512
    L_y3l = Last(2)
    L_scr = Last(1)
    L_xt = Last(1)
    L_A2 = Last(1)
    L_pj = Last(4)
    L_sqb = Last(2)
    L_wbf = Last(2)
    L_sgb = Last(1)
    L_tgu = Last(1)
    L_ys = Last(1)
    L_y4 = Last(1)
    L_tmb = Last(1)
    L_srow = Last(1)
    L_sbc = Last(1)
    L_shbc = Last(1)
    L_pss = Last(1)
    L_pssh = Last(1)
    ppj = 0

    for ti in range(NT2):
        tsl = slice(ti * 512, (ti + 1) * 512)
        cvo = []
        sum_last = sq_last = None
        for c in range(16):
            sl2 = c % 2
            ld = r.dma("sync", wstage[sl2][:, 0:512].bitcast(F32), y3[c * 128:(c + 1) * 128, tsl],
                       deps=([wlast[sl2]] if ti == 0 and c < 2 else []) + L_y3l.get(c),
                       stream=f"wst{sl2}")
            cv = r.op("vector", "tensor_copy", scr16[:, c * 512:(c + 1) * 512],
                      wstage[sl2][:, 0:512].bitcast(F32),
                      deps=[ld] + (L_scr.get(0) if c == 0 else []))
            L_y3l.set(c, cv)
            cvo.append(cv)
            dd = [cv]
            if c == 0:
                dd += [ld_ops[0]] + ctx["L_stats"].get(0)
            sum_last = r.op("tensor", "matmul", ps_sum[:, :], onesb_s[:, 0:1],
                            scr16[:, c * 512:(c + 1) * 512], start=(c == 0), stop=(c == 15),
                            deps=dd)
            sq = r.op("scalar", "activation", sqb[c % 2][:, :], scr16[:, c * 512:(c + 1) * 512],
                      AF.Square, deps=[cv] + L_sqb.get(c))
            dd2 = [sq]
            if c == 0:
                dd2 += ctx["L_stats"].get(0)
            sq_last = r.op("tensor", "matmul", ps_sq[:, :], onesb_s[:, 0:1], sqb[c % 2][:, :],
                           start=(c == 0), stop=(c == 15), deps=dd2)
            L_sqb.set(c, sq_last)
        bc1, bc2 = ln_rows(r, ctx, tsl, ps_sum, ps_sq, sum_last, sq_last, 1.0 / 2048.0,
                           "rows0", "rows1")
        ap_ops = []
        s_last = None
        for c in range(16):
            p1o = r.op("vector", "tensor_mul", xt[:, c * 512:(c + 1) * 512],
                       scr16[:, c * 512:(c + 1) * 512], ctx["rbc"][:, :],
                       deps=[bc1, cvo[c]] + L_xt.get(0))
            p2o = r.op("vector", "tensor_add", xt[:, c * 512:(c + 1) * 512],
                       xt[:, c * 512:(c + 1) * 512], ctx["nbc"][:, :], deps=[bc2, p1o])
            ap_ops.append(p2o)
            dd = [p2o, ld_ops[0]]
            if c == 0:
                dd += L_pss.get(0)
            s_last = r.op("tensor", "matmul", ps_s[:, :], onesb_s[:, 0:1],
                          xt[:, c * 512:(c + 1) * 512], start=(c == 0), stop=(c == 15), deps=dd)
        ctx["L_rows"].set(0, *ap_ops[-2:])
        sm = r.op("scalar", "mul", srow[:, :], ps_s[:, :], 128.0, deps=[s_last] + L_srow.get(0))
        L_pss.set(0, sm)
        sc = r.op("scalar", "copy", srow_bf[:, :], srow[:, :], deps=[sm])
        sstd = r.dma("gpsimd", srow_dram[0:1, tsl], srow_bf[:, :], deps=[sc], stream="sr0")
        sbcd = r.dma("gpsimd", sbc[:, :], srow_dram[0:1, tsl].broadcast_to((128, 512)),
                     deps=[sstd] + L_sbc.get(0), stream="sr0b")
        L_srow.set(0, sstd)

        h_deps = {}
        sh_last = None
        for grp in range(32):
            sl2 = grp % 2
            lw_gt = r.dma("sync", wbf[sl2].rearrange("p (kc j) -> p kc j", kc=16)[:, :, 0:256],
                          wg_bf[:, grp * 256:grp * 256 + 256].rearrange("(kc p) j -> p kc j", p=128),
                          deps=pre_done + L_wbf.get(sl2), stream=f"wb{sl2}")
            lw_up = r.dma("sync", wbf[sl2].rearrange("p (kc j) -> p kc j", kc=16)[:, :, 256:512],
                          wu_bf[:, grp * 256:grp * 256 + 256].rearrange("(kc p) j -> p kc j", p=128),
                          deps=pre_done, stream=f"wb{sl2}")
            wconsumers = []
            for sub in range(2):
                ic = grp * 2 + sub
                pg = pj[ppj]
                lastg = None
                for kc in range(16):
                    dd = []
                    if kc == 0:
                        dd = [lw_gt] + ap_ops + L_pj.get(ppj)
                    lastg = r.op("tensor", "matmul", pg[:, :],
                                 wbf[sl2][:, (kc * 4 + sub) * 128:(kc * 4 + sub) * 128 + 128],
                                 xt[:, kc * 512:(kc + 1) * 512],
                                 start=(kc == 0), stop=(kc == 15), deps=dd)
                ppg = ppj
                ppj = (ppj + 1) % 4
                pu = pj[ppj]
                lastu = None
                for kc in range(16):
                    dd = []
                    if kc == 0:
                        dd = [lw_up] + ap_ops + L_pj.get(ppj)
                    lastu = r.op("tensor", "matmul", pu[:, :],
                                 wbf[sl2][:, (kc * 4 + 2 + sub) * 128:(kc * 4 + 2 + sub) * 128 + 128],
                                 xt[:, kc * 512:(kc + 1) * 512],
                                 start=(kc == 0), stop=(kc == 15), deps=dd)
                ppu = ppj
                ppj = (ppj + 1) % 4
                tg = r.op("vector", "tensor_sub", tgu[:, :], pg[:, :], sbc[:, :],
                          deps=[lastg, sbcd] + L_tgu.get(0))
                sg = r.op("scalar", "activation", sgb[:, :], tgu[:, :], AF.Silu,
                          scale=gsc_s[:, ic:ic + 1], deps=[tg, ld_ops[1]] + L_sgb.get(0))
                tu = r.op("vector", "tensor_sub", tgu[:, :], pu[:, :], sbc[:, :],
                          deps=[lastu, sbcd, sg])
                hh = r.op("vector", "scalar_tensor_tensor", A2[:, ic * 512:(ic + 1) * 512],
                          tgu[:, :], usc_s[:, ic:ic + 1], sgb[:, :], ALU.mult, ALU.mult,
                          deps=[tu, sg, ld_ops[2]] + (L_A2.get(0) if ic == 0 else []))
                L_tgu.set(0, hh)
                L_pj.set(ppg, tg)
                L_pj.set(ppu, tu)
                L_sgb.set(0, hh)
                h_deps[ic] = hh
                dd = [hh, ld_ops[0]]
                if ic == 0:
                    dd += L_pssh.get(0)
                sh_last = r.op("tensor", "matmul", ps_sh[:, :], onesb_s[:, 0:1],
                               A2[:, ic * 512:(ic + 1) * 512],
                               start=(ic == 0), stop=(ic == 63), deps=dd)
                wconsumers.extend([lastg, lastu])
            L_wbf.set(sl2, *wconsumers)
        shm = r.op("scalar", "mul", srow[:, :], ps_sh[:, :], 128.0, deps=[sh_last, sstd] + L_srow.get(0))
        L_pssh.set(0, shm)
        shc = r.op("scalar", "copy", srow_bf[:, :], srow[:, :], deps=[shm])
        shstd = r.dma("gpsimd", srow_dram[1:2, tsl], srow_bf[:, :], deps=[shc], stream="sr1")
        shbcd = r.dma("gpsimd", shbc[:, :], srow_dram[1:2, tsl].broadcast_to((128, 512)),
                      deps=[shstd] + L_shbc.get(0), stream="sr1b")
        L_srow.set(0, shstd)

        sum_last = sq_last = None
        ffevs = []
        for m in range(16):
            sl2 = m % 2
            lw = r.dma("sync", wbf[sl2].rearrange("p (kc j) -> p kc j", kc=64, j=128),
                       wd_bf[:, m * 128:(m + 1) * 128].rearrange("(kc p) j -> p kc j", p=128),
                       deps=pre_done + L_wbf.get(sl2), stream=f"wb{sl2}")
            pu = pj[ppj]
            last = None
            for kc in range(64):
                dd = []
                if kc == 0:
                    dd = [lw] + L_pj.get(ppj) + [h_deps[i] for i in range(64)]
                last = r.op("tensor", "matmul", pu[:, :],
                            wbf[sl2][:, kc * 128:(kc + 1) * 128],
                            A2[:, kc * 512:(kc + 1) * 512],
                            start=(kc == 0), stop=(kc == 63), deps=dd)
            L_wbf.set(sl2, last)
            t1 = r.op("vector", "tensor_sub", tgu[:, :], pu[:, :], shbc[:, :],
                      deps=[last, shbcd] + L_tgu.get(0))
            ev = r.op("vector", "tensor_scalar", scr16[:, m * 512:(m + 1) * 512], tgu[:, :],
                      dsc_s[:, m:m + 1], None, ALU.mult,
                      deps=[t1, ld_ops[3]] + (L_scr.get(0) if m == 0 else []) + (cvo if m == 0 else []))
            L_tgu.set(0, ev)
            L_pj.set(ppj, t1)
            ppj = (ppj + 1) % 4
            ffevs.append(ev)
            sq = r.op("scalar", "activation", sqb[m % 2][:, :], scr16[:, m * 512:(m + 1) * 512],
                      AF.Square, deps=[ev] + L_sqb.get(m))
            dd = [ev]
            if m == 0:
                dd += [ld_ops[0]] + ctx["L_stats"].get(0)
            sum_last = r.op("tensor", "matmul", ps_sum[:, :], onesb_s[:, 0:1],
                            scr16[:, m * 512:(m + 1) * 512], start=(m == 0), stop=(m == 15),
                            deps=dd)
            dd2 = [sq]
            if m == 0:
                dd2 += ctx["L_stats"].get(0)
            sq_last = r.op("tensor", "matmul", ps_sq[:, :], onesb_s[:, 0:1], sqb[m % 2][:, :],
                           start=(m == 0), stop=(m == 15), deps=dd2)
            L_sqb.set(m, sq_last)
        bc1, bc2 = ln_rows(r, ctx, tsl, ps_sum, ps_sq, sum_last, sq_last, 1.0 / 2048.0,
                           "rows0", "rows1")
        scr_cons = []
        for m in range(16):
            ldy = r.dma("sync", ystage[m % 2][:, :], y3[m * 128:(m + 1) * 128, tsl],
                        deps=L_ys.get(m), stream="ysl")
            t1 = r.op("vector", "tensor_mul", tmb[:, :], scr16[:, m * 512:(m + 1) * 512],
                      ctx["rbc"][:, :], deps=[bc1, ffevs[m]] + L_tmb.get(0))
            t2 = r.op("vector", "tensor_add", tmb[:, :], tmb[:, :], ctx["nbc"][:, :],
                      deps=[bc2, t1])
            yo = r.op("vector", "tensor_add", y4t[m % 2][:, :], ystage[m % 2][:, :], tmb[:, :],
                      deps=[ldy, t2] + L_y4.get(m))
            sty = r.dma("sync", y4[m * 128:(m + 1) * 128, tsl], y4t[m % 2][:, :],
                        deps=[yo], stream="y4out")
            L_ys.set(m, yo)
            L_y4.set(m, sty)
            L_tmb.set(0, yo)
            scr_cons.append(t1)
        ctx["L_rows"].set(0, scr_cons[-1], scr_cons[-2])
        L_scr.set(0, *scr_cons)
        L_xt.set(0, s_last, h_deps[63])
        L_A2.set(0, ffevs[-1])
        L_sbc.set(0, h_deps[63])
        L_shbc.set(0, ffevs[-1])

    r.full_wait.update(["const"])
    ssems = open_stream_sems(nc, es, r)
    with nc.Block() as blk:
        r.finalize(blk, sems, ssems)
    es.close()
    return nc


def p6_inputs(full_inputs, y3_slices):
    t = make_tables()
    gsc = np.asarray(full_inputs["L1_ffn_gate_scale"]).reshape(64, 128).T.copy()
    usc = np.asarray(full_inputs["L1_ffn_up_scale"]).reshape(64, 128).T.copy()
    dsc = np.asarray(full_inputs["L1_ffn_down_scale"]).reshape(16, 128).T.copy()
    wgT = np.ascontiguousarray(np.asarray(full_inputs["L1_ffn_gate_w"]).T)
    wuT = np.ascontiguousarray(np.asarray(full_inputs["L1_ffn_up_w"]).T)
    wdT = np.ascontiguousarray(np.asarray(full_inputs["L1_ffn_down_w"]).T)
    maps = []
    for cid in range(NCORES):
        maps.append({
            "y3": np.ascontiguousarray(y3_slices[cid], dtype=np.float32),
            "wgT": wgT, "wuT": wuT, "wdT": wdT,
            "gsc": gsc.astype(np.float32), "usc": usc.astype(np.float32),
            "dsc": dsc.astype(np.float32), "ones_bf": t["ones_bf"],
        })
    return maps
